# version 23
# baseline (speedup 1.0000x reference)
import numpy as np
import ml_dtypes
import concourse.bass as bass
import concourse.mybir as mybir
from concourse.alu_op_type import AluOpType as ALU
from concourse.bass_utils import run_bass_kernel_spmd

F32 = mybir.dt.float32
BF16 = mybir.dt.bfloat16
AF = mybir.ActivationFunctionType

T = 8192
TOWN = 4096
DM = 2048
H = 16
DHD = 128
QKV = 6144
DFF = 5504
NSUB = 43
R9 = 72
NTILE = 64
NGRP = 16
GTOK = 512
NB = 8
EPS_LN = 1e-5




def _emit(nc, p):
    sems = {name: nc.alloc_semaphore(name) for name in ("ld", "gd", "sv", "ss", "st", "sg")}
    with nc.Block() as block:
        for engname, deco in (("sync", block.sync), ("gpsimd", block.gpsimd),
                              ("tensor", block.tensor), ("vector", block.vector),
                              ("scalar", block.scalar)):
            def emit(eng, engname=engname):
                for en, waits, fn, inc in p.events:
                    if en != engname:
                        continue
                    for s, v in waits:
                        eng.wait_ge(sems[s], v)
                    ins = fn(eng)
                    if inc is not None:
                        ins.then_inc(sems[inc[0]], inc[1])
            deco(emit)
    return sems


class P:

    def __init__(self):
        self.events = []
        self.ctr = {k: 0 for k in ("ld", "gd", "sv", "ss", "st", "sg")}

    def ev(self, eng, fn, waits=(), inc=None):
        for s, v in waits:
            assert v <= self.ctr[s], f"wait on future {s}={v} > {self.ctr[s]}"
        if inc is not None:
            self.ctr[inc[0]] += inc[1]
        self.events.append((eng, tuple(waits), fn, inc))
        return self.ctr[inc[0]] if inc else None

    def c(self, sem):
        return self.ctr[sem]


def build(n_groups=NGRP, do_phase_b=True, taps=(), tap_g=8):
    nc = bass.Bass("TRN2", target_bir_lowering=False, debug=False)
    taps = set(taps)

    d_xp = nc.dram_tensor("xp", [TOWN, DM], F32, kind="ExternalInput").ap()
    d_xo = nc.dram_tensor("xo", [TOWN, DM], F32, kind="ExternalInput").ap()
    d_wqkv = nc.dram_tensor("wqkv", [12, 128, 16, 512], F32, kind="ExternalInput").ap()
    d_wout = nc.dram_tensor("wout", [4, 128, 16, 512], F32, kind="ExternalInput").ap()
    d_wgad = nc.dram_tensor("wgad", [NSUB, 128, 2176], F32, kind="ExternalInput").ap()
    d_wuad = nc.dram_tensor("wuad", [NSUB, 128, 2176], F32, kind="ExternalInput").ap()
    d_wdn = nc.dram_tensor("wdn", [4, 128, NSUB, 512], F32, kind="ExternalInput").ap()
    d_acat_g = nc.dram_tensor("acat_g", [128, 16, R9], F32, kind="ExternalInput").ap()
    d_acat_u = nc.dram_tensor("acat_u", [128, 16, R9], F32, kind="ExternalInput").ap()
    d_bcat_d = nc.dram_tensor("bcat_d", [R9, 4, 512], F32, kind="ExternalInput").ap()
    d_acat_d = nc.dram_tensor("acat_d", [128, NSUB, R9], F32, kind="ExternalInput").ap()
    d_wrout = nc.dram_tensor("wrout", [128, 16, 8], F32, kind="ExternalInput").ap()
    d_ebias = nc.dram_tensor("ebias", [1, 8], F32, kind="ExternalInput").ap()
    d_dec = nc.dram_tensor("dec", [1, 16], F32, kind="ExternalInput").ap()
    d_idn_h = nc.dram_tensor("idn_h", [128, 128], BF16, kind="ExternalInput").ap()
    d_idn_f = nc.dram_tensor("idn_f", [128, 128], F32, kind="ExternalInput").ap()
    d_s8 = nc.dram_tensor("s8", [8, 64], BF16, kind="ExternalInput").ap()
    d_bsel = nc.dram_tensor("bsel", [1, 1], F32, kind="ExternalInput").ap()
    d_out = nc.dram_tensor("out", [TOWN, DM], F32, kind="ExternalOutput").ap()

    d_wqkv_h = nc.dram_tensor("wqkv_h", [12, 128, 16, 512], BF16).ap()
    d_wout_h = nc.dram_tensor("wout_h", [4, 128, 16, 512], BF16).ap()
    d_wgad_h = nc.dram_tensor("wgad_h", [NSUB, 128, 2176], BF16).ap()
    d_wuad_h = nc.dram_tensor("wuad_h", [NSUB, 128, 2176], BF16).ap()
    d_wdn_h = nc.dram_tensor("wdn_h", [4, 128, NSUB, 512], BF16).ap()
    d_ospill = nc.dram_tensor("ospill", [128, 32, 16, 128], BF16).ap()

    tap_aps = {}

    def tap(name, shape, dt=F32):
        if name in taps:
            tap_aps[name] = nc.dram_tensor("tap_" + name, shape, dt, kind="ExternalOutput").ap()
            return tap_aps[name]
        return None

    t_xn = tap("xn", [128, DM], BF16)
    t_q = tap("q", [128, 16, 128])
    t_phik = tap("phik", [128, GTOK])
    t_kv = tap("kv", [128, GTOK])
    t_S = tap("S", [128, 2 * GTOK])
    t_den = tap("den", [128, 4])
    t_o = tap("o", [128, 16, 128], BF16)
    t_x1 = tap("x1", [128, 4 * DM])
    t_xn2 = tap("xn2", [128, DM], BF16)
    t_rout = tap("router", [128, 4 * 8])
    t_h = tap("h", [128, NSUB * 512], BF16)
    t_gate = tap("gate", [128, 512])
    t_stm = tap("stm", [128, 256])
    t_dpre = tap("dpre", [128, 1])

    def sb(name, shape, dt):
        return nc.alloc_sbuf_tensor(name, shape, dt).ap()

    ones512 = sb("ones512", [128, 512], F32)
    idn_h = sb("idn_hs", [128, 128], BF16)
    idn_f = sb("idn_fs", [128, 128], F32)
    lamb = sb("lamb", [128, 16], F32)
    bselb = sb("bselb", [128, 1], F32)
    ebias1 = sb("ebias1", [1, 8], F32)
    ebias_b = sb("ebias_b", [128, 8], F32)
    eps_ln = sb("eps_ln", [128, 1], F32)

    acat_g_s = sb("acat_g_s", [128, 16, R9], BF16)
    acat_u_s = sb("acat_u_s", [128, 16, R9], BF16)
    bcat_d_s = sb("bcat_d_s", [R9, 4, 512], BF16)
    wrout_s = sb("wrout_s", [128, 16, 8], BF16)
    acat_d_s = sb("acat_d_s", [128, NSUB, R9], BF16)
    s8_s = sb("s8_s", [8, 64], BF16)
    sraw = sb("sraw", [128, 96], F32)
    stat = sb("stat", [128, 8], F32)
    lnv4 = sb("lnv4", [128, 4], F32)
    rstd4 = sb("rstd4", [128, 4], F32)
    nb4 = sb("nb4", [128, 4], F32)

    import contextlib
    stA = contextlib.ExitStack()

    def sba(name, shape, dt):
        return stA.enter_context(nc.sbuf_tensor(name, shape, dt)).ap()

    carry = sba("carry", [128, 32], F32)
    dec = sba("dec_s", [1, 16], F32)
    lamv = sba("lamv", [1, 16], F32)
    onesK1 = sba("onesK1", [1, 128], F32)
    bsel_s = sba("bsel_s", [1, 1], F32)
    x_t = [sba(f"x_t{i}", [128, DM], F32) for i in range(4)]
    xn_t = [sba(f"xn_t{i}", [128, DM], BF16) for i in range(2)]
    xn_fm = [sba(f"xn_fm{i}", [128, 16, GTOK], BF16) for i in range(2)]
    wkv = [sba(f"wkv{i}", [128, 16, 512], BF16) for i in range(2)]

    r_buf = [sba(f"r_buf{i}", [128, 512], F32) for i in range(2)]
    m_buf = [sba(f"m_buf{i}", [128, 512], F32) for i in range(2)]
    e_buf = [sba(f"e_buf{i}", [128, 512], F32) for i in range(2)]
    X_ks = [sba(f"X_ks{i}", [128, 512], F32) for i in range(4)]
    X_kv = [sba(f"X_kv{i}", [128, 512], F32) for i in range(4)]
    S_fm = [sba(f"S_fm{i}", [128, 1024], F32) for i in range(4)]
    lam_t = [sba(f"lam_t{i}", [128, 512], F32) for i in range(2)]
    q_st = sba("q_st", [128, 8192], BF16)
    den4 = sba("den4", [128, 8], F32)
    dinv4 = sba("dinv4", [128, 8], F32)
    o_blk4 = sba("o_blk4", [128, 512], BF16)
    scr4 = sba("scr4", [128, 512], F32)
    o_stg = sba("o_stg", [128, 4, 16, 128], BF16)
    tden_b = sba("tden_b", [128, 4], F32) if t_den is not None else None

    ps = [nc.alloc_psum_tensor(f"ps{i}", [128, 512], F32).ap() for i in range(8)]
    ps_bf = [p.bitcast(BF16) for p in ps]

    p = P()
    E = p.ev

    ld0 = p.c("ld")
    E("sync", lambda e: e.dma_start(out=idn_h[:], in_=d_idn_h), inc=("ld", 16))
    E("sync", lambda e: e.dma_start(out=idn_f[:], in_=d_idn_f), inc=("ld", 16))
    E("sync", lambda e: e.dma_start(out=dec[:], in_=d_dec), inc=("ld", 16))
    E("sync", lambda e: e.dma_start(out=ebias1[:], in_=d_ebias), inc=("ld", 16))
    ld_setup = p.c("ld")

    cast_done = {}
    gd_smallw = None

    E("vector", lambda e: e.memset(ones512[:], 1.0))
    E("vector", lambda e: e.memset(eps_ln[:], EPS_LN))
    E("vector", lambda e: e.memset(onesK1[:], 1.0), inc=("sv", 1))
    sv_ones = p.c("sv")
    E("scalar", lambda e: e.activation(out=lamv[:], in_=dec[:], func=AF.Exp, scale=-1.0),
      waits=[("ld", ld_setup)], inc=("ss", 1))
    ss_l1 = p.c("ss")
    E("vector", lambda e: e.tensor_scalar(out=lamv[:], in0=lamv[:], scalar1=1.0,
                                          scalar2=None, op0=ALU.add),
      waits=[("ss", ss_l1)], inc=("sv", 1))
    sv_l2 = p.c("sv")
    E("vector", lambda e: e.reciprocal(lamv[:], lamv[:]), waits=[("sv", sv_l2)], inc=("sv", 1))
    sv_l3 = p.c("sv")
    E("sync", lambda e: e.dma_start(out=bsel_s[:], in_=d_bsel), inc=("ld", 16))
    ld_bsel = p.c("ld")
    E("tensor", lambda e: e.matmul(ps[7][:, 0:16], onesK1[:], lamv[:], start=True, stop=False),
      waits=[("sv", sv_l3)], inc=None)
    E("tensor", lambda e: e.matmul(ps[7][:, 16:17], onesK1[:], bsel_s[:], start=False, stop=False),
      waits=[("ld", ld_bsel)], inc=None)
    E("tensor", lambda e: e.matmul(ps[7][:, 24:32], onesK1[:], ebias1[:], start=False, stop=True),
      waits=[("ld", ld_setup)], inc=("st", 1))
    st_bc = p.c("st")
    E("vector", lambda e: e.tensor_copy(lamb[:], ps[7][:, 0:16]), waits=[("st", st_bc)])
    E("vector", lambda e: e.tensor_copy(bselb[:], ps[7][:, 16:17]))
    E("vector", lambda e: e.tensor_copy(ebias_b[:], ps[7][:, 24:32]), inc=("sv", 1))
    sv_bc = p.c("sv")

    def upd(tr, *new):
        return list(new) + [w for w in tr if w[0] in ("ld", "gd")]

    xt_free = [[], [], [], []]
    xnt_free = [[], []]
    wkv_free = [[], []]
    Xks_free = [[] for _ in range(4)]
    Xkv_free = [[] for _ in range(4)]
    Sfm_free = [[] for _ in range(4)]
    lam_free = [[], []]
    rme_free = [[], []]
    ostage_free = [[]]
    xnfm_ready_grp = {}
    kbank_free = {2: [], 3: []}
    vbank_free = {4: [], 5: []}
    tbank_free = {0: [], 1: []}
    sbank_free = {6: [], 7: []}
    head_tts = {}
    phik_ready = {}
    qst_free = []

    ACT_SET_A = "natural_log_exp_and_others"

    xnfm_free = [[], []]
    ln_ready = {}

    def emit_ln_group(g):
        xf = g % 2
        for i in range(4):
            tile = 4 * g + i
            slot = i
            src = d_xp if tile < 32 else d_xo
            row0 = (tile % 32) * 128
            w_old = list(xt_free[slot])
            E("sync", lambda e, s=slot, src=src, r=row0: e.dma_start(
                out=x_t[s][:], in_=src[r:r + 128, :]), waits=w_old, inc=("ld", 16))
            ldx = p.c("ld")
            for j in range(4):
                E("vector", lambda e, s=slot, i=i, j=j: e.bn_stats(
                    sraw[:, i * 24 + j * 6: i * 24 + j * 6 + 6],
                    x_t[s][:, j * 512:(j + 1) * 512]), waits=[("ld", ldx)] if j == 0 else [],
                  inc=("sv", 1) if j == 3 else None)
            sv_bn = p.c("sv")
            E("vector", lambda e, i=i: e.bn_aggr(stat[:, i * 2:i * 2 + 2],
                                                 sraw[:, i * 24:(i + 1) * 24]),
              waits=[("sv", sv_bn)], inc=("sv", 1))
        sv_aggr = p.c("sv")
        E("scalar", lambda e: e.activation(out=lnv4[:], in_=stat[:].rearrange("p (t k) -> p k t", k=2)[:, 1, :],
                                           func=AF.Ln, bias=eps_ln[:]),
          waits=[("sv", sv_aggr), ("sv", sv_ones)], inc=("ss", 1))
        ss_ln = p.c("ss")
        E("scalar", lambda e: e.activation(out=rstd4[:], in_=lnv4[:], func=AF.Exp, scale=-0.5),
          waits=[("ss", ss_ln)], inc=("ss", 1))
        ss_rstd = p.c("ss")
        E("vector", lambda e: e.scalar_tensor_tensor(
            out=nb4[:], in0=stat[:].rearrange("p (t k) -> p k t", k=2)[:, 0, :],
            scalar=-1.0, in1=rstd4[:], op0=ALU.mult, op1=ALU.mult),
          waits=[("ss", ss_rstd)], inc=("sv", 1))
        sv_nb = p.c("sv")
        for i in range(4):
            tile = 4 * g + i
            slot = tile % 2
            E("scalar", lambda e, s=slot, xs=i, i=i: e.activation(
                out=xn_t[s][:], in_=x_t[xs][:], func=AF.Identity,
                scale=rstd4[:, i:i + 1], bias=nb4[:, i:i + 1]),
              waits=[("sv", sv_nb)] + xnt_free[slot], inc=("ss", 1))
            ss_xn = p.c("ss")
            xt_free[i] = upd(xt_free[i], ("ss", ss_xn))
            if t_xn is not None and tile == 4 * tap_g:
                E("sync", lambda e, s=slot: e.dma_start(out=t_xn, in_=xn_t[s][:]),
                  waits=[("ss", ss_xn)], inc=("ld", 16))
                xnt_free[slot] = xnt_free[slot] + [("ld", p.c("ld"))]
            for half in range(2):
                bank = half
                w = [("ss", ss_xn)] + tbank_free[bank]
                for kc8 in range(8):
                    kc = half * 8 + kc8
                    E("tensor", lambda e, s=slot, kc=kc, b=bank, kc8=kc8: e.transpose(
                        ps_bf[b][:, kc8 * 128:(kc8 + 1) * 128],
                        xn_t[s][:, kc * 128:(kc + 1) * 128], idn_h[:]),
                      waits=w if kc8 == 0 else [],
                      inc=("st", 1) if kc8 == 7 else None)
                st_tp = p.c("st")
                E("vector", lambda e, i=i, half=half, b=bank, xf=xf: e.tensor_copy(
                    xn_fm[xf][:, half * 8:(half + 1) * 8, i * 128:(i + 1) * 128], ps_bf[b][:]),
                  waits=[("st", st_tp)] + (xnfm_free[xf] if (i == 0 and half == 0) else []),
                  inc=("sv", 1))
                tbank_free[bank] = upd(tbank_free[bank], ("sv", p.c("sv")))
            xnt_free[slot] = upd(xnt_free[slot], ("st", st_tp))
        ln_ready[g] = p.c("sv")

    emit_ln_group(0)

    deno_q = []

    def emit_deno(h, sfs, g):
            nonlocal oblk_free
            sv_ready = head_tts[h]
            hp = h % 2
            st_pair = []
            for pi in range(2):
                bank = 6 + pi
                w2 = [("sv", sv_ready)] + sbank_free[bank]
                for bi in range(4):
                    i = 2 * pi + bi // 2
                    half = bi % 2
                    E("tensor", lambda e, sfs=sfs, i=i, half=half, b=bank, bi=bi: e.transpose(
                        ps[b][:, bi * 128:(bi + 1) * 128],
                        S_fm[sfs][:, half * 512 + i * 128: half * 512 + (i + 1) * 128],
                        idn_f[:]),
                      waits=w2 if bi == 0 else [],
                      inc=("st", 1) if bi == 3 else None)
                st_pair.append(p.c("st"))
            Sfm_free[sfs] = upd(Sfm_free[sfs], ("st", st_pair[-1]))
            for i in range(4):
                bank = 6 + i // 2
                ks_off = (2 * (i % 2) + 1) * 128
                qo = (i * 16 + h) * 128
                E("vector", lambda e, qo=qo, b=bank, ko=ks_off, i=i: e.tensor_tensor(
                    out=scr4[:, i * 128:(i + 1) * 128], in0=q_st[:, qo:qo + 128],
                    in1=ps[b][:, ko:ko + 128], op=ALU.mult),
                  waits=([("st", st_pair[0]), ("st", st_pair[1]),
                          ("sv", sg_q_grp)] if i == 0 else []),
                  inc=("sv", 1) if i == 3 else None)
            sv_tt4 = p.c("sv")
            for i in range(4):
                E("vector", lambda e, i=i, hp=hp: e.tensor_reduce(
                    out=den4[:, hp * 4 + i:hp * 4 + i + 1],
                    in_=scr4[:, i * 128:(i + 1) * 128],
                    axis=mybir.AxisListType.X, op=ALU.add),
                  waits=[("sv", sv_tt4)] if i == 0 else [],
                  inc=("sv", 1) if i == 3 else None)
            sv_r4 = p.c("sv")
            E("vector", lambda e, hp=hp: e.tensor_scalar(
                out=den4[:, hp * 4:hp * 4 + 4], in0=den4[:, hp * 4:hp * 4 + 4],
                scalar1=1e-6, scalar2=None, op0=ALU.max),
              waits=[("sv", sv_r4)], inc=("sv", 1))
            sv_c4 = p.c("sv")
            E("vector", lambda e, hp=hp: e.reciprocal(
                dinv4[:, hp * 4:hp * 4 + 4], den4[:, hp * 4:hp * 4 + 4]),
              waits=[("sv", sv_c4)], inc=("sv", 1))
            sv_i4 = p.c("sv")
            if t_den is not None and g == tap_g and h == 3:
                E("vector", lambda e, hp=hp: e.tensor_copy(tden_b[:], dinv4[:, hp * 4:hp * 4 + 4]),
                  waits=[("sv", sv_i4)], inc=("sv", 1))
            for i in range(4):
                bank = 6 + i // 2
                kv_off = (2 * (i % 2)) * 128
                qo = (i * 16 + h) * 128
                E("vector", lambda e, qo=qo, b=bank, ko=kv_off, i=i, hp=hp: e.scalar_tensor_tensor(
                    out=o_blk4[:, i * 128:(i + 1) * 128], in0=q_st[:, qo:qo + 128],
                    scalar=dinv4[:, hp * 4 + i:hp * 4 + i + 1],
                    in1=ps[b][:, ko:ko + 128], op0=ALU.mult, op1=ALU.mult),
                  waits=([("sv", sv_i4)] + oblk_free) if i == 0 else [],
                  inc=("sv", 1) if i == 3 else None)
            sv_o4 = p.c("sv")
            sbank_free[6] = upd(sbank_free[6], ("sv", sv_o4))
            sbank_free[7] = upd(sbank_free[7], ("sv", sv_o4))
            obank = h % 2
            w3 = [("sv", sv_o4)] + tbank_free[obank]
            for i in range(4):
                E("tensor", lambda e, i=i, b=obank: e.transpose(
                    ps_bf[b][:, i * 128:(i + 1) * 128],
                    o_blk4[:, i * 128:(i + 1) * 128], idn_h[:]),
                  waits=w3 if i == 0 else [],
                  inc=("st", 1) if i == 3 else None)
            st_ot = p.c("st")
            oblk_free = upd(oblk_free, ("st", st_ot))
            E("vector", lambda e, h=h, b=obank: e.tensor_copy(
                o_stg[:, :, h, :], ps_bf[b][:, 0:512].rearrange(
                    "p (a c) -> p a c", a=4)),
              waits=[("st", st_ot)] + ostage_free[0], inc=("sv", 1))
            tbank_free[obank] = upd(tbank_free[obank], ("sv", p.c("sv")))


    for g in range(n_groups):
        own = g >= 8
        xf = g % 2
        sv_xnfm = ln_ready[g]
        piece_seq = []
        if own:
            for j in range(4):
                piece_seq.append(("q", j))
        for j in range(4):
            piece_seq.append(("k", 4 + j))
            piece_seq.append(("v", 8 + j))

        if g == 1:
            for pc in (0, 1, 2, 3):
                E("gpsimd", lambda e, pc=pc: e.dma_start(out=d_wqkv_h[pc], in_=d_wqkv[pc]),
                  inc=("gd", 16))
                cast_done[("qkv", pc)] = p.c("gd")
        if g == 2 and do_phase_b:
            for nm, dst, src2 in (("wout", d_wout_h, d_wout), ("wgad", d_wgad_h, d_wgad),
                                  ("wuad", d_wuad_h, d_wuad), ("wdn", d_wdn_h, d_wdn)):
                E("gpsimd", lambda e, dst=dst, src2=src2: e.dma_start(out=dst[:], in_=src2[:]),
                  inc=("gd", 16))
                cast_done[nm] = p.c("gd")
            for dst, src2 in ((acat_g_s, d_acat_g), (acat_u_s, d_acat_u),
                              (bcat_d_s, d_bcat_d), (wrout_s, d_wrout), (acat_d_s, d_acat_d)):
                E("gpsimd", lambda e, dst=dst, src2=src2: e.dma_start(out=dst[:], in_=src2[:]),
                  inc=("gd", 16))
            E("gpsimd", lambda e: e.dma_start(out=s8_s[:], in_=d_s8), inc=("gd", 16))
            gd_smallw = p.c("gd")
        if g + 1 < n_groups:
            emit_ln_group(g + 1)
        wslot = 0
        sg_q_grp = p.c("sv")
        oblk_free = []
        for kind, pidx in piece_seq:
            wslot ^= 1
            if g < 2:
                E("gpsimd", lambda e, s=wslot, pc=pidx: e.dma_start(
                    out=wkv[s][:], in_=d_wqkv[pc]),
                  waits=list(wkv_free[wslot]), inc=("gd", 16))
                ldw_sem, ldw = "gd", p.c("gd")
                if g == 0:
                    E("gpsimd", lambda e, pc=pidx: e.dma_start(out=d_wqkv_h[pc], in_=d_wqkv[pc]),
                      inc=("gd", 16))
                    cast_done[("qkv", pidx)] = p.c("gd")
                elif g == 1 and pidx in (0, 1, 2, 3) and ("qkv", pidx) not in cast_done:
                    E("gpsimd", lambda e, pc=pidx: e.dma_start(out=d_wqkv_h[pc], in_=d_wqkv[pc]),
                      inc=("gd", 16))
                    cast_done[("qkv", pidx)] = p.c("gd")
            else:
                if pidx % 2 == 1:
                    E("gpsimd", lambda e, s=wslot, pc=pidx: e.dma_start(
                        out=wkv[s][:], in_=d_wqkv_h[pc]),
                      waits=[("gd", cast_done[("qkv", pidx)])] + wkv_free[wslot],
                      inc=("gd", 16))
                    ldw_sem, ldw = "gd", p.c("gd")
                else:
                    E("sync", lambda e, s=wslot, pc=pidx: e.dma_start(
                        out=wkv[s][:], in_=d_wqkv_h[pc]),
                      waits=[("gd", cast_done[("qkv", pidx)])] + wkv_free[wslot],
                      inc=("ld", 16))
                    ldw_sem, ldw = "ld", p.c("ld")
            if kind == "k":
                for s4 in range(4):
                    h = (pidx - 4) * 4 + s4
                    bank = 2 + (h % 2)
                    rme = h % 2
                    w = [(ldw_sem, ldw), ("sv", sv_xnfm)] + kbank_free[bank]
                    for kc in range(16):
                        E("tensor", lambda e, s=wslot, kc=kc, s4=s4, b=bank, xf=xf: e.matmul(
                            ps[b][:], wkv[s][:, kc, s4 * 128:(s4 + 1) * 128],
                            xn_fm[xf][:, kc, :], start=(kc == 0), stop=(kc == 15)),
                          waits=w if kc == 0 else [], inc=("st", 1) if kc == 15 else None)
                    st_k = p.c("st")
                    E("scalar", lambda e, b=bank, r=rme: e.activation(
                        out=m_buf[r][:], in_=ps[b][:], func=AF.Relu, scale=-1.0),
                      waits=[("st", st_k)] + rme_free[rme], inc=None)
                    E("scalar", lambda e, b=bank, r=rme: e.activation(
                        out=r_buf[r][:], in_=ps[b][:], func=AF.Relu),
                      inc=("ss", 1))
                    ss_r = p.c("ss")
                    kbank_free[bank] = upd(kbank_free[bank], ("ss", ss_r))
                    E("scalar", lambda e, r=rme: e.activation(
                        out=e_buf[r][:], in_=m_buf[r][:], func=AF.Exp, scale=-1.0),
                      waits=[("ss", ss_r)], inc=("ss", 1))
                    ss_e = p.c("ss")
                    hs = h % 4
                    E("vector", lambda e, r=rme, hs=hs: e.tensor_add(
                        X_ks[hs][:], e_buf[r][:], r_buf[r][:]),
                      waits=[("ss", ss_e)] + Xks_free[hs], inc=("sv", 1))
                    sv_phik = p.c("sv")
                    phik_ready[h] = sv_phik
                    rme_free[rme] = upd(rme_free[rme], ("sv", sv_phik))
                    if t_phik is not None and g == tap_g and h == 3:
                        E("gpsimd", lambda e, hs=hs: e.dma_start(out=t_phik, in_=X_ks[hs][:]),
                          waits=[("sv", sv_phik)], inc=("gd", 16))
                        Xks_free[hs] = Xks_free[hs] + [("gd", p.c("gd"))]
                wkv_free[wslot] = upd(wkv_free[wslot], ("st", p.c("st")))
            elif kind == "v":
                for s4 in range(4):
                    h = (pidx - 8) * 4 + s4
                    hs = h % 4
                    bank = 4 + (h % 2)
                    w = [(ldw_sem, ldw), ("sv", sv_xnfm)] + vbank_free[bank]
                    for kc in range(16):
                        E("tensor", lambda e, s=wslot, kc=kc, s4=s4, b=bank, xf=xf: e.matmul(
                            ps[b][:], wkv[s][:, kc, s4 * 128:(s4 + 1) * 128],
                            xn_fm[xf][:, kc, :], start=(kc == 0), stop=(kc == 15)),
                          waits=w if kc == 0 else [], inc=("st", 1) if kc == 15 else None)
                    st_v = p.c("st")
                    E("vector", lambda e, b=bank, hs=hs: e.tensor_tensor(
                        out=X_kv[hs][:], in0=X_ks[hs][:], in1=ps[b][:], op=ALU.mult),
                      waits=[("st", st_v), ("sv", phik_ready[h])] + Xkv_free[hs],
                      inc=("sv", 1))
                    sv_kv = p.c("sv")
                    vbank_free[bank] = upd(vbank_free[bank], ("sv", sv_kv))
                    if t_kv is not None and g == tap_g and h == 3:
                        E("sync", lambda e, hs=hs: e.dma_start(out=t_kv, in_=X_kv[hs][:]),
                          waits=[("sv", sv_kv)], inc=("ld", 16))
                        Xkv_free[hs] = Xkv_free[hs] + [("ld", p.c("ld"))]
                    hslot = h % 2
                    sfs = h % 4
                    if g == 8:
                        E("vector", lambda e, hl=hslot, h=h: e.tensor_scalar(
                            out=lam_t[hl][:], in0=ones512[:], scalar1=lamb[:, h:h + 1],
                            scalar2=None, op0=ALU.mult),
                          waits=lam_free[hslot], inc=("sv", 1))
                        sv_lam = p.c("sv")
                        E("vector", lambda e, hl=hslot: e.tensor_tensor(
                            out=lam_t[hl][:, 0:1], in0=lam_t[hl][:, 0:1], in1=bselb[:],
                            op=ALU.mult),
                          waits=[("sv", sv_lam)], inc=("sv", 1))
                        sv_lam = p.c("sv")
                        data0 = lam_t[hslot][:]
                        w_tts = [("sv", sv_lam)] + Sfm_free[sfs]
                    else:
                        data0 = lamb[:, h:h + 1].to_broadcast((128, 512))
                        w_tts = ([("sv", sv_bc)] if g == 0 else []) + list(Sfm_free[sfs])
                    if g > 0:
                        w_tts.append(("sv", head_tts[h]))
                    if g == 0:
                        E("vector", lambda e, d0=data0, hs=hs, sfs=sfs: e.tensor_tensor_scan(
                            out=S_fm[sfs][:, 0:512], data0=d0, data1=X_kv[hs][:],
                            initial=0.0, op0=ALU.mult, op1=ALU.add), waits=w_tts)
                        E("vector", lambda e, d0=data0, hs=hs, sfs=sfs: e.tensor_tensor_scan(
                            out=S_fm[sfs][:, 512:1024], data0=d0, data1=X_ks[hs][:],
                            initial=0.0, op0=ALU.mult, op1=ALU.add), inc=("sv", 1))
                    else:
                        E("vector", lambda e, d0=data0, hs=hs, sfs=sfs, h=h: e.tensor_tensor_scan(
                            out=S_fm[sfs][:, 0:512], data0=d0, data1=X_kv[hs][:],
                            initial=carry[:, 2 * h:2 * h + 1], op0=ALU.mult, op1=ALU.add),
                          waits=w_tts)
                        E("vector", lambda e, d0=data0, hs=hs, sfs=sfs, h=h: e.tensor_tensor_scan(
                            out=S_fm[sfs][:, 512:1024], data0=d0, data1=X_ks[hs][:],
                            initial=carry[:, 2 * h + 1:2 * h + 2], op0=ALU.mult, op1=ALU.add),
                          inc=("sv", 1))
                    sv_tts = p.c("sv")
                    if g == 8:
                        lam_free[hslot] = upd(lam_free[hslot], ("sv", sv_tts))
                    Xkv_free[hs] = upd(Xkv_free[hs], ("sv", sv_tts))
                    Xks_free[hs] = upd(Xks_free[hs], ("sv", sv_tts))
                    E("vector", lambda e, sfs=sfs, h=h: e.tensor_copy(
                        carry[:, 2 * h:2 * h + 2],
                        S_fm[sfs][:].rearrange("p (a b) -> p a b", a=2)[:, :, 511]),
                      waits=[("sv", sv_tts)], inc=("sv", 1))
                    head_tts[h] = p.c("sv")
                    if t_S is not None and g == tap_g and h == 3:
                        E("sync", lambda e, sfs=sfs: e.dma_start(out=t_S, in_=S_fm[sfs][:]),
                          waits=[("sv", head_tts[h])], inc=("ld", 16))
                        Sfm_free[sfs] = Sfm_free[sfs] + [("ld", p.c("ld"))]
                    if not own:
                        Sfm_free[sfs] = upd(Sfm_free[sfs], ("sv", head_tts[h]))
                    else:
                        deno_q.append((h, sfs, g))
                        if len(deno_q) > 2:
                            emit_deno(*deno_q.pop(0))
                wkv_free[wslot] = upd(wkv_free[wslot], ("st", p.c("st")))
            else:
                pq = pidx
                for i in range(4):
                    bank = 4 + (i % 2)
                    rme = i % 2
                    w = [(ldw_sem, ldw), ("sv", sv_xnfm)] + vbank_free[bank] + qst_free
                    for kc in range(16):
                        E("tensor", lambda e, s=wslot, kc=kc, i=i, b=bank, xf=xf: e.matmul(
                            ps[b][:], xn_fm[xf][:, kc, i * 128:(i + 1) * 128],
                            wkv[s][:, kc, :], start=(kc == 0), stop=(kc == 15)),
                          waits=w if kc == 0 else [], inc=("st", 1) if kc == 15 else None)
                    st_q = p.c("st")
                    E("scalar", lambda e, b=bank, r=rme: e.activation(
                        out=m_buf[r][:], in_=ps[b][:], func=AF.Relu, scale=-1.0),
                      waits=[("st", st_q)] + rme_free[rme], inc=None)
                    E("scalar", lambda e, b=bank, r=rme: e.activation(
                        out=r_buf[r][:], in_=ps[b][:], func=AF.Relu),
                      inc=("ss", 1))
                    ss_r = p.c("ss")
                    vbank_free[bank] = upd(vbank_free[bank], ("ss", ss_r))
                    E("scalar", lambda e, r=rme: e.activation(
                        out=e_buf[r][:], in_=m_buf[r][:], func=AF.Exp, scale=-1.0),
                      waits=[("ss", ss_r)], inc=("ss", 1))
                    ss_e = p.c("ss")
                    qoff = (i * 16 + 4 * pq) * 128
                    E("vector", lambda e, r=rme, qo=qoff: e.tensor_add(
                        q_st[:, qo:qo + 512], e_buf[r][:], r_buf[r][:]),
                      waits=[("ss", ss_e)] + qst_free, inc=("sv", 1))
                    rme_free[rme] = upd(rme_free[rme], ("sv", p.c("sv")))
                wkv_free[wslot] = upd(wkv_free[wslot], ("st", p.c("st")))
                sg_q_grp = p.c("sv")

        while deno_q:
            emit_deno(*deno_q.pop(0))
        xnfm_free[xf] = [("st", p.c("st"))]
        if own:
            if t_q is not None and g == tap_g:
                E("sync", lambda e: e.dma_start(out=t_q, in_=q_st[:, 0:2048]),
                  waits=[("sv", sg_q_grp)], inc=("ld", 16))
                qst_free = qst_free + [("ld", p.c("ld"))]
            qst_free = upd(qst_free, ("sv", p.c("sv")))
            if t_den is not None and g == tap_g:
                E("sync", lambda e: e.dma_start(out=t_den, in_=tden_b[:]),
                  waits=[("sv", p.c("sv"))], inc=("ld", 16))
            E("gpsimd", lambda e, g=g: e.dma_start(
                out=d_ospill[:, (g - 8) * 4:(g - 8) * 4 + 4, :, :], in_=o_stg[:]),
              waits=[("sv", p.c("sv"))], inc=("gd", 16))
            ostage_free[0] = upd(ostage_free[0], ("gd", p.c("gd")))
            if t_o is not None and g == tap_g:
                E("gpsimd", lambda e: e.dma_start(out=t_o, in_=o_stg[:, 0, :, :]),
                  inc=("gd", 16))

    if not do_phase_b:
        E("sync", lambda e: e.nop(), waits=[("sv", p.c("sv")), ("ss", p.c("ss")),
                                                   ("st", p.c("st")), ("sg", p.c("sg")),
                                                   ("gd", p.c("gd")), ("ld", p.c("ld"))])
        _emit(nc, p)
        return nc, p, tap_aps

    bar = [("sv", p.c("sv")), ("ss", p.c("ss")), ("st", p.c("st")),
           ("sg", p.c("sg")), ("gd", p.c("gd")), ("ld", p.c("ld"))]
    for eng in ("sync", "gpsimd", "tensor", "vector", "scalar"):
        E(eng, lambda e: e.nop(), waits=bar)
    stA.close()

    xb = [sb(f"xb{j}", [128, DM], F32) for j in range(4)]
    o_fm = sb("o_fm", [128, 4, 16, 128], BF16)
    xn2_t = sb("xn2_t", [128, DM], BF16)
    xn2_fm = sb("xn2_fm", [128, 16, 512], BF16)
    h_buf = sb("h_buf", [128, NSUB, 512], BF16)
    wout_b = [sb(f"wout_b{i}", [128, 16, 512], BF16) for i in range(2)]
    wga = [sb(f"wga{i}", [128, 2176], BF16) for i in range(2)]
    wua = [sb(f"wua{i}", [128, 2176], BF16) for i in range(2)]
    wdn_b = [sb(f"wdn_b{i}", [128, 11, 512], BF16) for i in range(2)]
    sg_b = sb("sg_b", [128, 512], F32)
    Pg = sg_b[0:R9, :]
    Pgs = sb("Pgs", [R9, 512], BF16)
    Pus = sb("Pus", [R9, 512], BF16)
    Pds = sb("Pds", [R9, 512], BF16)
    w_em = sb("w_em", [8, 512], BF16)
    w_exp = sb("w_exp", [64, 512], BF16)
    sraw2, stat2, lnv4b, rstd4b, nb4b = sraw, stat, lnv4, rstd4, nb4
    sc8 = sb("sc8", [128, 8], F32)
    bi8 = sb("bi8", [128, 8], F32)
    mk8 = sb("mk8", [128, 8], F32)
    tm8 = sb("tm8", [128, 8], F32)
    m1b = sb("m1b", [128, 1], F32)
    m2b = sb("m2b", [128, 1], F32)
    wsum = sb("wsum", [128, 1], F32)
    wtok = sb("wtok", [128, 8], BF16)

    xb_free = [[] for _ in range(4)]
    ofm_free = []
    woutb_free = [[], []]
    wga_free = [[], []]
    wua_free = [[], []]
    wdnb_free = [[], []]
    xn2fm_free = []
    hbuf_free = []
    pfree = {i: [] for i in range(8)}

    ld_o = None
    ldx_b = None

    def emit_b_loads(b):
        nonlocal ld_o, ldx_b
        tiles = [4 * b + j for j in range(4)]
        E("sync", lambda e, b=b: e.dma_start(
            out=o_fm[:], in_=d_ospill[:, 4 * b:4 * b + 4, :, :]),
          waits=list(ofm_free), inc=("ld", 16))
        ld_o = p.c("ld")
        ldx_b = []
        for j in range(4):
            E("sync", lambda e, j=j, t=tiles[j]: e.dma_start(
                out=xb[j][:], in_=d_xo[t * 128:(t + 1) * 128, :]),
              waits=list(xb_free[j]), inc=("ld", 16))
            ldx_b.append(p.c("ld"))

    emit_b_loads(0)
    for b in range(NB):
        tiles = [4 * b + j for j in range(4)]
        ld_o_b, ldx_b_b = ld_o, ldx_b
        sv_x1 = []
        for fo in range(4):
            ws = fo % 2
            E("sync", lambda e, ws=ws, fo=fo: e.dma_start(
                out=wout_b[ws][:], in_=d_wout_h[fo]),
              waits=[("gd", cast_done["wout"])] + woutb_free[ws], inc=("ld", 16))
            ld_w = p.c("ld")
            for j in range(4):
                bank = 4 + (j % 2)
                w = [("ld", ld_w), ("ld", ld_o_b), ("ld", ldx_b_b[j])] + pfree[bank]
                for kc in range(16):
                    E("tensor", lambda e, j=j, kc=kc, ws=ws, bk=bank: e.matmul(
                        ps[bk][:], o_fm[:, j, kc, :], wout_b[ws][:, kc, :],
                        start=(kc == 0), stop=(kc == 15)),
                      waits=w if kc == 0 else [], inc=("st", 1) if kc == 15 else None)
                st_w = p.c("st")
                E("vector", lambda e, j=j, fo=fo, bk=bank: e.tensor_tensor(
                    out=xb[j][:, fo * 512:(fo + 1) * 512], in0=ps[bk][:],
                    in1=xb[j][:, fo * 512:(fo + 1) * 512], op=ALU.add),
                  waits=[("st", st_w)], inc=("sv", 1))
                pfree[bank] = upd(pfree[bank], ("sv", p.c("sv")))
                sv_x1.append(p.c("sv"))
            woutb_free[ws] = upd(woutb_free[ws], ("st", p.c("st")))
        sv_x1_all = p.c("sv")
        if t_x1 is not None and b == 0:
            for j in range(4):
                E("sync", lambda e, j=j: e.dma_start(out=t_x1[:, j * DM:(j + 1) * DM],
                                                     in_=xb[j][:]),
                  waits=[("sv", sv_x1_all)], inc=("ld", 16))
                xb_free[j] = xb_free[j] + [("ld", p.c("ld"))]
        for j in range(4):
            for q4 in range(4):
                E("vector", lambda e, j=j, q4=q4: e.bn_stats(
                    sraw2[:, j * 24 + q4 * 6: j * 24 + q4 * 6 + 6],
                    xb[j][:, q4 * 512:(q4 + 1) * 512]),
                  waits=[("sv", sv_x1_all)] if (j == 0 and q4 == 0) else [],
                  inc=("sv", 1) if q4 == 3 else None)
            sv_bn = p.c("sv")
            E("vector", lambda e, j=j: e.bn_aggr(stat2[:, j * 2:j * 2 + 2],
                                                 sraw2[:, j * 24:(j + 1) * 24]),
              waits=[("sv", sv_bn)], inc=("sv", 1))
        sv_aggr = p.c("sv")
        E("scalar", lambda e: e.activation(
            out=lnv4b[:], in_=stat2[:].rearrange("p (t k) -> p k t", k=2)[:, 1, :],
            func=AF.Ln, bias=eps_ln[:]),
          waits=[("sv", sv_aggr)], inc=("ss", 1))
        ss_ln = p.c("ss")
        E("scalar", lambda e: e.activation(out=rstd4b[:], in_=lnv4b[:], func=AF.Exp,
                                           scale=-0.5),
          waits=[("ss", ss_ln)], inc=("ss", 1))
        ss_rstd = p.c("ss")
        E("vector", lambda e: e.scalar_tensor_tensor(
            out=nb4b[:], in0=stat2[:].rearrange("p (t k) -> p k t", k=2)[:, 0, :],
            scalar=-1.0, in1=rstd4b[:], op0=ALU.mult, op1=ALU.mult),
          waits=[("ss", ss_rstd)], inc=("sv", 1))
        sv_nb = p.c("sv")
        sv_rout = []
        for j in range(4):
            E("scalar", lambda e, j=j: e.activation(
                out=xn2_t[:], in_=xb[j][:], func=AF.Identity,
                scale=rstd4b[:, j:j + 1], bias=nb4b[:, j:j + 1]),
              waits=[("sv", sv_nb), ("st", p.c("st"))], inc=("ss", 1))
            ss_xn2 = p.c("ss")
            if t_xn2 is not None and b == 0 and j == 0:
                E("sync", lambda e: e.dma_start(out=t_xn2, in_=xn2_t[:]),
                  waits=[("ss", ss_xn2)], inc=("ld", 16))
            for half in range(2):
                bank = half
                w = [("ss", ss_xn2)] + pfree[bank]
                for kc8 in range(8):
                    kc = half * 8 + kc8
                    E("tensor", lambda e, kc=kc, bk=bank, kc8=kc8: e.transpose(
                        ps_bf[bk][:, kc8 * 128:(kc8 + 1) * 128],
                        xn2_t[:, kc * 128:(kc + 1) * 128], idn_h[:]),
                      waits=w if kc8 == 0 else [],
                      inc=("st", 1) if kc8 == 7 else None)
                st_tp = p.c("st")
                E("vector", lambda e, j=j, half=half, bk=bank: e.tensor_copy(
                    xn2_fm[:, half * 8:(half + 1) * 8, j * 128:(j + 1) * 128],
                    ps_bf[bk][:]),
                  waits=[("st", st_tp)] + xn2fm_free, inc=("sv", 1))
                pfree[bank] = upd(pfree[bank], ("sv", p.c("sv")))
        sv_xn2fm = p.c("sv")
        xn2fm_free = []
        for j in range(4):
            w = [("sv", sv_xn2fm), ("gd", gd_smallw)] + pfree[7]
            for kc in range(16):
                E("tensor", lambda e, j=j, kc=kc: e.matmul(
                    ps[7][:, 0:8], xn2_fm[:, kc, j * 128:(j + 1) * 128],
                    wrout_s[:, kc, :], start=(kc == 0), stop=(kc == 15)),
                  waits=w if kc == 0 else [], inc=("st", 1) if kc == 15 else None)
            st_r = p.c("st")
            E("scalar", lambda e: e.activation(out=sc8[:], in_=ps[7][:, 0:8],
                                               func=AF.Exp, scale=-1.0),
              waits=[("st", st_r)], inc=("ss", 1))
            ss_e8 = p.c("ss")
            pfree[7] = upd(pfree[7], ("ss", ss_e8))
            E("vector", lambda e: e.tensor_scalar(out=sc8[:], in0=sc8[:], scalar1=1.0,
                                                  scalar2=None, op0=ALU.add),
              waits=[("ss", ss_e8)], inc=("sv", 1))
            E("vector", lambda e: e.reciprocal(sc8[:], sc8[:]),
              waits=[("sv", p.c("sv"))], inc=("sv", 1))
            E("vector", lambda e: e.tensor_add(bi8[:], sc8[:], ebias_b[:]),
              waits=[("sv", p.c("sv"))], inc=("sv", 1))
            E("vector", lambda e: e.tensor_reduce(out=m1b[:], in_=bi8[:],
                                                  axis=mybir.AxisListType.X, op=ALU.max),
              waits=[("sv", p.c("sv"))], inc=("sv", 1))
            E("vector", lambda e: e.tensor_scalar(out=mk8[:], in0=bi8[:], scalar1=m1b[:],
                                                  scalar2=None, op0=ALU.is_ge),
              waits=[("sv", p.c("sv"))], inc=("sv", 1))
            E("vector", lambda e: e.scalar_tensor_tensor(
                out=tm8[:], in0=mk8[:], scalar=-1e30, in1=bi8[:],
                op0=ALU.mult, op1=ALU.add),
              waits=[("sv", p.c("sv"))], inc=("sv", 1))
            E("vector", lambda e: e.tensor_reduce(out=m2b[:], in_=tm8[:],
                                                  axis=mybir.AxisListType.X, op=ALU.max),
              waits=[("sv", p.c("sv"))], inc=("sv", 1))
            E("vector", lambda e: e.tensor_scalar(out=mk8[:], in0=bi8[:], scalar1=m2b[:],
                                                  scalar2=None, op0=ALU.is_ge),
              waits=[("sv", p.c("sv"))], inc=("sv", 1))
            E("vector", lambda e: e.tensor_tensor(out=mk8[:], in0=sc8[:], in1=mk8[:],
                                                  op=ALU.mult),
              waits=[("sv", p.c("sv"))], inc=("sv", 1))
            E("vector", lambda e: e.tensor_reduce(out=wsum[:], in_=mk8[:],
                                                  axis=mybir.AxisListType.X, op=ALU.add),
              waits=[("sv", p.c("sv"))], inc=("sv", 1))
            E("vector", lambda e: e.tensor_scalar(out=wsum[:], in0=wsum[:], scalar1=1e-8,
                                                  scalar2=None, op0=ALU.add),
              waits=[("sv", p.c("sv"))], inc=("sv", 1))
            E("vector", lambda e: e.reciprocal(wsum[:], wsum[:]),
              waits=[("sv", p.c("sv"))], inc=("sv", 1))
            E("vector", lambda e: e.tensor_scalar(out=wtok[:], in0=mk8[:], scalar1=wsum[:],
                                                  scalar2=None, op0=ALU.mult),
              waits=[("sv", p.c("sv"))], inc=("sv", 1))
            sv_wt = p.c("sv")
            if t_rout is not None and b == 0:
                E("gpsimd", lambda e, j=j: e.dma_start(out=t_rout[:, j * 8:(j + 1) * 8],
                                                       in_=wtok[:]),
                  waits=[("sv", sv_wt)], inc=("gd", 16))
            E("tensor", lambda e: e.transpose(ps_bf[7][0:8, 0:128], wtok[:], idn_h[:]),
              waits=[("sv", sv_wt)] + pfree[7], inc=("st", 1))
            st_wt = p.c("st")
            E("vector", lambda e, j=j: e.tensor_copy(w_em[:, j * 128:(j + 1) * 128],
                                                     ps_bf[7][0:8, 0:128]),
              waits=[("st", st_wt)], inc=("sv", 1))
            pfree[7] = upd(pfree[7], ("sv", p.c("sv")))
            sv_rout.append(p.c("sv"))
        sv_wem = p.c("sv")
        E("tensor", lambda e: e.matmul(ps[6][0:64, :], s8_s[:], w_em[:],
                                       start=True, stop=True),
          waits=[("sv", sv_wem), ("gd", gd_smallw)] + pfree[6], inc=("st", 1))
        st_we = p.c("st")
        E("vector", lambda e: e.tensor_copy(w_exp[:], ps[6][0:64, :]),
          waits=[("st", st_we)], inc=("sv", 1))
        sv_wexp = p.c("sv")
        pfree[6] = upd(pfree[6], ("sv", sv_wexp))
        for (acat, Pst, nm) in ((acat_g_s, Pgs, "g"), (acat_u_s, Pus, "u")):
            w = [("sv", sv_xn2fm)] + pfree[6]
            for kc in range(16):
                E("tensor", lambda e, kc=kc, acat=acat: e.matmul(
                    ps[6][0:R9, :], acat[:, kc, :], xn2_fm[:, kc, :],
                    start=(kc == 0), stop=(kc == 15)),
                  waits=w if kc == 0 else [], inc=("st", 1) if kc == 15 else None)
            st_p = p.c("st")
            E("scalar", lambda e: e.activation(out=Pg[:], in_=ps[6][0:R9, :],
                                               func=AF.Identity, scale=0.125),
              waits=[("st", st_p), ("sv", p.c("sv"))], inc=("ss", 1))
            ss_p = p.c("ss")
            pfree[6] = upd(pfree[6], ("ss", ss_p))
            E("vector", lambda e, Pst=Pst: e.tensor_tensor(
                out=Pst[0:64, :], in0=Pg[0:64, :], in1=w_exp[:], op=ALU.mult),
              waits=[("ss", ss_p), ("sv", sv_wexp)], inc=("sv", 1))
            E("vector", lambda e, Pst=Pst: e.tensor_copy(Pst[64:R9, :], Pg[64:R9, :]),
              inc=("sv", 1))
        sv_pl = p.c("sv")
        sv_h_last = 0
        pd_pending = []
        for s in range(NSUB):
            wsl = s % 2
            E("sync", lambda e, s=s, w=wsl: e.dma_start(out=wga[w][:], in_=d_wgad_h[s]),
              waits=[("gd", cast_done["wgad"])] + wga_free[wsl], inc=("ld", 16))
            ld_g = p.c("ld")
            E("sync", lambda e, s=s, w=wsl: e.dma_start(out=wua[w][:], in_=d_wuad_h[s]),
              waits=[("gd", cast_done["wuad"])] + wua_free[wsl], inc=("ld", 16))
            ld_u = p.c("ld")
            w = [("ld", ld_g), ("sv", sv_pl)] + pfree[2]
            for kc in range(16):
                E("tensor", lambda e, kc=kc, w2=wsl: e.matmul(
                    ps[2][:], wga[w2][:, kc * 128:(kc + 1) * 128], xn2_fm[:, kc, :],
                    start=(kc == 0), stop=False),
                  waits=w if kc == 0 else [])
            E("tensor", lambda e, w2=wsl: e.matmul(
                ps[2][:], wga[w2][0:R9, 2048:2176], Pgs[:],
                start=False, stop=True), inc=("st", 1))
            st_g = p.c("st")
            wga_free[wsl] = upd(wga_free[wsl], ("st", st_g))
            E("scalar", lambda e: e.activation(out=sg_b[:], in_=ps[2][:], func=AF.Silu),
              waits=[("st", st_g)], inc=("ss", 1))
            ss_sg = p.c("ss")
            pfree[2] = upd(pfree[2], ("ss", ss_sg))
            if t_gate is not None and b == 0 and s == 0:
                E("sync", lambda e: e.dma_start(out=t_gate, in_=sg_b[:]),
                  waits=[("ss", ss_sg)], inc=("ld", 16))
            w = [("ld", ld_u), ("sv", sv_pl)] + pfree[3]
            for kc in range(16):
                E("tensor", lambda e, kc=kc, w2=wsl: e.matmul(
                    ps[3][:], wua[w2][:, kc * 128:(kc + 1) * 128], xn2_fm[:, kc, :],
                    start=(kc == 0), stop=False),
                  waits=w if kc == 0 else [])
            E("tensor", lambda e, w2=wsl: e.matmul(
                ps[3][:], wua[w2][0:R9, 2048:2176], Pus[:],
                start=False, stop=True), inc=("st", 1))
            st_u = p.c("st")
            E("vector", lambda e, s=s: e.tensor_tensor(
                out=h_buf[:, s, :], in0=sg_b[:], in1=ps[3][:], op=ALU.mult),
              waits=[("st", st_u), ("ss", ss_sg)] + (hbuf_free if s == 0 else []),
              inc=("sv", 1))
            sv_h = p.c("sv")
            pfree[3] = upd(pfree[3], ("sv", sv_h))
            sv_h_last = sv_h
            pd_pending.append((s, wsl, sv_h))
            if len(pd_pending) > 1 or s == NSUB - 1:
                todo = pd_pending if s == NSUB - 1 else [pd_pending.pop(0)]
                for (s0, w0, svh0) in todo:
                    E("tensor", lambda e, s0=s0: e.matmul(
                        ps[6][0:R9, :], acat_d_s[:, s0, :], h_buf[:, s0, :],
                        start=(s0 == 0), stop=(s0 == NSUB - 1)),
                      waits=[("sv", svh0)] + (pfree[6] if s0 == 0 else []),
                      inc=("st", 1))
                if s == NSUB - 1:
                    pd_pending.clear()
            wua_free[wsl] = upd(wua_free[wsl], ("st", p.c("st")))
        st_pd = p.c("st")
        if t_h is not None and b == 0:
            E("sync", lambda e: e.dma_start(out=t_h, in_=h_buf[:].rearrange("p a b -> p (a b)")),
              waits=[("sv", sv_h_last)], inc=("ld", 16))
            hbuf_free = hbuf_free + [("ld", p.c("ld"))]
        E("scalar", lambda e: e.activation(out=Pg[:], in_=ps[6][0:R9, :],
                                           func=AF.Identity, scale=0.125),
          waits=[("st", st_pd)], inc=("ss", 1))
        ss_pd = p.c("ss")
        pfree[6] = upd(pfree[6], ("ss", ss_pd))
        E("vector", lambda e: e.tensor_tensor(out=Pds[0:64, :], in0=Pg[0:64, :],
                                              in1=w_exp[:], op=ALU.mult),
          waits=[("ss", ss_pd)], inc=("sv", 1))
        E("vector", lambda e: e.tensor_copy(Pds[64:R9, :], Pg[64:R9, :]), inc=("sv", 1))
        sv_pds = p.c("sv")
        kq_ranges = [(0, 11), (11, 22), (22, 33), (33, NSUB)]
        for fo in range(4):
            bankset = [2, 3, 4, 5] if fo % 2 == 0 else [6, 7, 0, 1]
            for kq, (a0, a1) in enumerate(kq_ranges):
                wsl = kq % 2
                E("sync", lambda e, fo=fo, a0=a0, a1=a1, w=wsl: e.dma_start(
                    out=wdn_b[w][:, 0:a1 - a0, :], in_=d_wdn_h[fo][:, a0:a1, :]),
                  waits=[("gd", cast_done["wdn"])] + wdnb_free[wsl], inc=("ld", 16))
                ld_d = p.c("ld")
                for j in range(4):
                    bank = bankset[j]
                    w = [("ld", ld_d), ("sv", sv_h_last)]
                    if kq == 0:
                        w += pfree[bank]
                    nk = a1 - a0
                    for ki, kc in enumerate(range(a0, a1)):
                        E("tensor", lambda e, kc=kc, ki=ki, j=j, bk=bank, w2=wsl, fo=fo: e.matmul(
                            ps[bk][:], h_buf[:, kc, j * 128:(j + 1) * 128],
                            wdn_b[w2][:, ki, :],
                            start=(kc == 0), stop=False),
                          waits=w if ki == 0 else [],
                          inc=("st", 1) if ki == nk - 1 else None)
                    if kq == 3:
                        E("tensor", lambda e, j=j, bk=bank, fo=fo: e.matmul(
                            ps[bk][:], Pds[:, j * 128:(j + 1) * 128], bcat_d_s[:, fo, :],
                            start=False, stop=True),
                          waits=[("sv", sv_pds)], inc=("st", 1))
                wdnb_free[wsl] = upd(wdnb_free[wsl], ("st", p.c("st")))
            st_dn = p.c("st")
            for j in range(4):
                bank = bankset[j]
                E("vector", lambda e, j=j, fo=fo, bk=bank: e.tensor_tensor(
                    out=xb[j][:, fo * 512:(fo + 1) * 512], in0=ps[bk][:],
                    in1=xb[j][:, fo * 512:(fo + 1) * 512], op=ALU.add),
                  waits=[("st", st_dn)], inc=("sv", 1))
                pfree[bank] = upd(pfree[bank], ("sv", p.c("sv")))
                if fo == 3:
                    E("gpsimd", lambda e, j=j, t=tiles[j]: e.dma_start(
                        out=d_out[t * 128:(t + 1) * 128, :], in_=xb[j][:]),
                      waits=[("sv", p.c("sv"))], inc=("gd", 16))
                    xb_free[j] = upd(xb_free[j], ("gd", p.c("gd")))
        sv_out = p.c("sv")
        ofm_free = upd(ofm_free, ("st", p.c("st")))
        xn2fm_free = [("st", p.c("st"))]
        hbuf_free = upd(hbuf_free, ("st", p.c("st")))
        if b + 1 < NB:
            emit_b_loads(b + 1)

    E("sync", lambda e: e.nop(), waits=[("sv", p.c("sv")), ("ss", p.c("ss")),
                                        ("st", p.c("st")), ("sg", p.c("sg")),
                                        ("gd", p.c("gd")), ("ld", p.c("ld"))])
    _emit(nc, p)
    return nc, p, tap_aps



def _prep_inputs(L0_attn_x, L0_attn_w_qkv, L0_attn_w_out, L0_attn_decay_param,
                 L0_ffn_W_router, L0_ffn_W_gate, L0_ffn_W_up, L0_ffn_W_down,
                 L0_ffn_sA_gate, L0_ffn_sB_gate, L0_ffn_sA_up, L0_ffn_sB_up,
                 L0_ffn_sA_down, L0_ffn_sB_down, L0_ffn_A_gate, L0_ffn_B_gate,
                 L0_ffn_A_up, L0_ffn_B_up, L0_ffn_A_down, L0_ffn_B_down,
                 L0_ffn_expert_bias):
    f = np.ascontiguousarray
    x = np.asarray(L0_attn_x, np.float32)
    wqkv = np.asarray(L0_attn_w_qkv, np.float32).reshape(16, 128, 12, 512).transpose(2, 1, 0, 3)
    wout = np.asarray(L0_attn_w_out, np.float32).reshape(16, 128, 4, 512).transpose(2, 1, 0, 3)
    wg = np.asarray(L0_ffn_W_gate, np.float32).reshape(16, 128, NSUB, 128).transpose(2, 1, 0, 3)
    wu = np.asarray(L0_ffn_W_up, np.float32).reshape(16, 128, NSUB, 128).transpose(2, 1, 0, 3)
    wdn = np.asarray(L0_ffn_W_down, np.float32).reshape(NSUB, 128, 4, 512).transpose(2, 1, 0, 3)

    acat_g = np.concatenate([np.asarray(L0_ffn_A_gate, np.float32).transpose(1, 0, 2).reshape(DM, 64),
                             np.asarray(L0_ffn_sA_gate, np.float32)], 1).reshape(16, 128, R9).transpose(1, 0, 2)
    acat_u = np.concatenate([np.asarray(L0_ffn_A_up, np.float32).transpose(1, 0, 2).reshape(DM, 64),
                             np.asarray(L0_ffn_sA_up, np.float32)], 1).reshape(16, 128, R9).transpose(1, 0, 2)
    acat_d = np.concatenate([np.asarray(L0_ffn_A_down, np.float32).transpose(1, 0, 2).reshape(DFF, 64),
                             np.asarray(L0_ffn_sA_down, np.float32)], 1)
    bcat_g = np.concatenate([np.asarray(L0_ffn_B_gate, np.float32).reshape(64, DFF),
                             np.asarray(L0_ffn_sB_gate, np.float32)], 0)
    bcat_u = np.concatenate([np.asarray(L0_ffn_B_up, np.float32).reshape(64, DFF),
                             np.asarray(L0_ffn_sB_up, np.float32)], 0)
    bcat_d = np.concatenate([np.asarray(L0_ffn_B_down, np.float32).reshape(64, DM),
                             np.asarray(L0_ffn_sB_down, np.float32)], 0).reshape(R9, 4, 512)

    acat_d_s = acat_d.reshape(NSUB, 128, R9)
    acat_d_s_host = acat_d_s.transpose(1, 0, 2)
    bcat_g_s = bcat_g.reshape(R9, NSUB, 128)
    bcat_u_s = bcat_u.reshape(R9, NSUB, 128)
    wgad = np.zeros((NSUB, 128, 2176), np.float32)
    wuad = np.zeros((NSUB, 128, 2176), np.float32)
    wgad[:, :, 0:2048] = wg.reshape(NSUB, 128, 2048)
    wuad[:, :, 0:2048] = wu.reshape(NSUB, 128, 2048)
    wgad[:, 0:R9, 2048:2176] = bcat_g_s.transpose(1, 0, 2)
    wuad[:, 0:R9, 2048:2176] = bcat_u_s.transpose(1, 0, 2)

    wrout = np.asarray(L0_ffn_W_router, np.float32).reshape(16, 128, 8).transpose(1, 0, 2)
    ebias = np.asarray(L0_ffn_expert_bias, np.float32).reshape(1, 8)
    dec = np.asarray(L0_attn_decay_param, np.float32).reshape(1, 16)
    idn_h = np.eye(128, dtype=ml_dtypes.bfloat16)
    idn_f = np.eye(128, dtype=np.float32)
    s8 = np.zeros((8, 64), ml_dtypes.bfloat16)
    for e in range(8):
        s8[e, e * 8:(e + 1) * 8] = 1

    shared = dict(wqkv=f(wqkv), wout=f(wout), wgad=f(wgad), wuad=f(wuad), wdn=f(wdn),
                  acat_g=f(acat_g), acat_u=f(acat_u), bcat_d=f(bcat_d), wrout=f(wrout),
                  acat_d=f(acat_d_s_host),
                  ebias=f(ebias), dec=f(dec), idn_h=idn_h, idn_f=idn_f, s8=s8)
    zeros_half = np.zeros((TOWN, DM), np.float32)
    in_maps = []
    for c in range(8):
        b, hh = c // 2, c % 2
        m = dict(shared)
        m["xo"] = f(x[b, hh * TOWN:(hh + 1) * TOWN])
        m["xp"] = f(x[b, 0:TOWN]) if hh == 1 else zeros_half
        m["bsel"] = np.full((1, 1), float(hh), np.float32)
        in_maps.append(m)
    return in_maps


def kernel(**inputs):
    in_maps = _prep_inputs(**inputs)
    nc, _, _ = build()
    res = run_bass_kernel_spmd(nc, in_maps, core_ids=list(range(8)))
    out = np.zeros((4, T, DM), np.float32)
    for c in range(8):
        b, hh = c // 2, c % 2
        out[b, hh * TOWN:(hh + 1) * TOWN] = np.asarray(res.results[c]["out"])
    return out


# version 24
# speedup vs baseline: 1.0630x; 1.0630x over previous
import numpy as np
import ml_dtypes
import concourse.bass as bass
import concourse.mybir as mybir
from concourse.alu_op_type import AluOpType as ALU
from concourse.bass_utils import run_bass_kernel_spmd

F32 = mybir.dt.float32
BF16 = mybir.dt.bfloat16
AF = mybir.ActivationFunctionType

T = 8192
TOWN = 4096
DM = 2048
H = 16
DHD = 128
QKV = 6144
DFF = 5504
NSUB = 43
R9 = 72
NTILE = 64
NGRP = 16
GTOK = 512
NB = 8
EPS_LN = 1e-5




def _emit(nc, p):
    sems = {name: nc.alloc_semaphore(name) for name in ("ld", "gd", "sv", "ss", "st", "sg")}
    with nc.Block() as block:
        for engname, deco in (("sync", block.sync), ("gpsimd", block.gpsimd),
                              ("tensor", block.tensor), ("vector", block.vector),
                              ("scalar", block.scalar)):
            def emit(eng, engname=engname):
                for en, waits, fn, inc in p.events:
                    if en != engname:
                        continue
                    for s, v in waits:
                        eng.wait_ge(sems[s], v)
                    ins = fn(eng)
                    if inc is not None:
                        ins.then_inc(sems[inc[0]], inc[1])
            deco(emit)
    return sems


class P:

    def __init__(self):
        self.events = []
        self.ctr = {k: 0 for k in ("ld", "gd", "sv", "ss", "st", "sg")}

    def ev(self, eng, fn, waits=(), inc=None):
        for s, v in waits:
            assert v <= self.ctr[s], f"wait on future {s}={v} > {self.ctr[s]}"
        if inc is not None:
            self.ctr[inc[0]] += inc[1]
        self.events.append((eng, tuple(waits), fn, inc))
        return self.ctr[inc[0]] if inc else None

    def c(self, sem):
        return self.ctr[sem]


def build(n_groups=NGRP, do_phase_b=True, taps=(), tap_g=8):
    nc = bass.Bass("TRN2", target_bir_lowering=False, debug=False)
    taps = set(taps)

    d_xp = nc.dram_tensor("xp", [TOWN, DM], F32, kind="ExternalInput").ap()
    d_xo = nc.dram_tensor("xo", [TOWN, DM], F32, kind="ExternalInput").ap()
    d_wqkv = nc.dram_tensor("wqkv", [12, 128, 16, 512], F32, kind="ExternalInput").ap()
    d_wout = nc.dram_tensor("wout", [4, 128, 16, 512], F32, kind="ExternalInput").ap()
    d_wgad = nc.dram_tensor("wgad", [NSUB, 128, 2176], F32, kind="ExternalInput").ap()
    d_wuad = nc.dram_tensor("wuad", [NSUB, 128, 2176], F32, kind="ExternalInput").ap()
    d_wdn = nc.dram_tensor("wdn", [4, 128, NSUB, 512], F32, kind="ExternalInput").ap()
    d_acat_g = nc.dram_tensor("acat_g", [128, 16, R9], F32, kind="ExternalInput").ap()
    d_acat_u = nc.dram_tensor("acat_u", [128, 16, R9], F32, kind="ExternalInput").ap()
    d_bcat_d = nc.dram_tensor("bcat_d", [R9, 4, 512], F32, kind="ExternalInput").ap()
    d_acat_d = nc.dram_tensor("acat_d", [128, NSUB, R9], F32, kind="ExternalInput").ap()
    d_wrout = nc.dram_tensor("wrout", [128, 16, 8], F32, kind="ExternalInput").ap()
    d_ebias = nc.dram_tensor("ebias", [1, 8], F32, kind="ExternalInput").ap()
    d_dec = nc.dram_tensor("dec", [1, 16], F32, kind="ExternalInput").ap()
    d_idn_h = nc.dram_tensor("idn_h", [128, 128], BF16, kind="ExternalInput").ap()
    d_idn_f = nc.dram_tensor("idn_f", [128, 128], F32, kind="ExternalInput").ap()
    d_s8 = nc.dram_tensor("s8", [8, 64], BF16, kind="ExternalInput").ap()
    d_bsel = nc.dram_tensor("bsel", [1, 1], F32, kind="ExternalInput").ap()
    d_out = nc.dram_tensor("out", [TOWN, DM], F32, kind="ExternalOutput").ap()

    d_wqkv_h = nc.dram_tensor("wqkv_h", [12, 128, 16, 512], BF16).ap()
    d_wout_h = nc.dram_tensor("wout_h", [4, 128, 16, 512], BF16).ap()
    d_wgad_h = nc.dram_tensor("wgad_h", [NSUB, 128, 2176], BF16).ap()
    d_wuad_h = nc.dram_tensor("wuad_h", [NSUB, 128, 2176], BF16).ap()
    d_wdn_h = nc.dram_tensor("wdn_h", [4, 128, NSUB, 512], BF16).ap()
    d_ospill = nc.dram_tensor("ospill", [128, 32, 16, 128], BF16).ap()

    tap_aps = {}

    def tap(name, shape, dt=F32):
        if name in taps:
            tap_aps[name] = nc.dram_tensor("tap_" + name, shape, dt, kind="ExternalOutput").ap()
            return tap_aps[name]
        return None

    t_xn = tap("xn", [128, DM], BF16)
    t_q = tap("q", [128, 16, 128])
    t_phik = tap("phik", [128, GTOK])
    t_kv = tap("kv", [128, GTOK])
    t_S = tap("S", [128, 2 * GTOK])
    t_den = tap("den", [128, 4])
    t_o = tap("o", [128, 16, 128], BF16)
    t_x1 = tap("x1", [128, 4 * DM])
    t_xn2 = tap("xn2", [128, DM], BF16)
    t_rout = tap("router", [128, 4 * 8])
    t_h = tap("h", [128, NSUB * 512], BF16)
    t_gate = tap("gate", [128, 512])
    t_stm = tap("stm", [128, 256])
    t_dpre = tap("dpre", [128, 1])

    def sb(name, shape, dt):
        return nc.alloc_sbuf_tensor(name, shape, dt).ap()

    ones512 = sb("ones512", [128, 512], F32)
    idn_h = sb("idn_hs", [128, 128], BF16)
    idn_f = sb("idn_fs", [128, 128], F32)
    lamb = sb("lamb", [128, 16], F32)
    bselb = sb("bselb", [128, 1], F32)
    ebias1 = sb("ebias1", [1, 8], F32)
    ebias_b = sb("ebias_b", [128, 8], F32)
    eps_ln = sb("eps_ln", [128, 1], F32)

    acat_g_s = sb("acat_g_s", [128, 16, R9], BF16)
    acat_u_s = sb("acat_u_s", [128, 16, R9], BF16)
    bcat_d_s = sb("bcat_d_s", [R9, 4, 512], BF16)
    wrout_s = sb("wrout_s", [128, 16, 8], BF16)
    acat_d_s = sb("acat_d_s", [128, NSUB, R9], BF16)
    s8_s = sb("s8_s", [8, 64], BF16)
    sraw = sb("sraw", [128, 96], F32)
    stat = sb("stat", [128, 8], F32)
    lnv4 = sb("lnv4", [128, 4], F32)
    rstd4 = sb("rstd4", [128, 4], F32)
    nb4 = sb("nb4", [128, 4], F32)

    import contextlib
    stA = contextlib.ExitStack()

    def sba(name, shape, dt):
        return stA.enter_context(nc.sbuf_tensor(name, shape, dt)).ap()

    carry = sba("carry", [128, 32], F32)
    dec = sba("dec_s", [1, 16], F32)
    lamv = sba("lamv", [1, 16], F32)
    onesK1 = sba("onesK1", [1, 128], F32)
    bsel_s = sba("bsel_s", [1, 1], F32)
    x_t = [sba(f"x_t{i}", [128, DM], F32) for i in range(4)]
    xn_t = [sba(f"xn_t{i}", [128, DM], BF16) for i in range(2)]
    xn_fm = [sba(f"xn_fm{i}", [128, 16, GTOK], BF16) for i in range(2)]
    wkv = [sba(f"wkv{i}", [128, 16, 512], BF16) for i in range(2)]

    r_buf = [sba(f"r_buf{i}", [128, 512], F32) for i in range(2)]
    m_buf = [sba(f"m_buf{i}", [128, 512], F32) for i in range(2)]
    e_buf = [sba(f"e_buf{i}", [128, 512], F32) for i in range(2)]
    X_ks = [sba(f"X_ks{i}", [128, 512], F32) for i in range(4)]
    X_kv = [sba(f"X_kv{i}", [128, 512], F32) for i in range(4)]
    S_fm = [sba(f"S_fm{i}", [128, 1024], F32) for i in range(4)]
    lam_t = [sba(f"lam_t{i}", [128, 512], F32) for i in range(2)]
    q_st = sba("q_st", [128, 8192], BF16)
    den4 = sba("den4", [128, 8], F32)
    dinv4 = sba("dinv4", [128, 8], F32)
    o_blk4 = sba("o_blk4", [128, 512], BF16)
    scr4 = sba("scr4", [128, 512], F32)
    o_stg = sba("o_stg", [128, 4, 16, 128], BF16)
    tden_b = sba("tden_b", [128, 4], F32) if t_den is not None else None

    ps = [nc.alloc_psum_tensor(f"ps{i}", [128, 512], F32).ap() for i in range(8)]
    ps_bf = [p.bitcast(BF16) for p in ps]

    p = P()
    E = p.ev

    ld0 = p.c("ld")
    E("sync", lambda e: e.dma_start(out=idn_h[:], in_=d_idn_h), inc=("ld", 16))
    E("sync", lambda e: e.dma_start(out=idn_f[:], in_=d_idn_f), inc=("ld", 16))
    E("sync", lambda e: e.dma_start(out=dec[:], in_=d_dec), inc=("ld", 16))
    E("sync", lambda e: e.dma_start(out=ebias1[:], in_=d_ebias), inc=("ld", 16))
    ld_setup = p.c("ld")

    cast_done = {}
    gd_smallw = None

    E("vector", lambda e: e.memset(ones512[:], 1.0))
    E("vector", lambda e: e.memset(eps_ln[:], EPS_LN))
    E("vector", lambda e: e.memset(onesK1[:], 1.0), inc=("sv", 1))
    sv_ones = p.c("sv")
    E("scalar", lambda e: e.activation(out=lamv[:], in_=dec[:], func=AF.Exp, scale=-1.0),
      waits=[("ld", ld_setup)], inc=("ss", 1))
    ss_l1 = p.c("ss")
    E("vector", lambda e: e.tensor_scalar(out=lamv[:], in0=lamv[:], scalar1=1.0,
                                          scalar2=None, op0=ALU.add),
      waits=[("ss", ss_l1)], inc=("sv", 1))
    sv_l2 = p.c("sv")
    E("vector", lambda e: e.reciprocal(lamv[:], lamv[:]), waits=[("sv", sv_l2)], inc=("sv", 1))
    sv_l3 = p.c("sv")
    E("sync", lambda e: e.dma_start(out=bsel_s[:], in_=d_bsel), inc=("ld", 16))
    ld_bsel = p.c("ld")
    E("tensor", lambda e: e.matmul(ps[7][:, 0:16], onesK1[:], lamv[:], start=True, stop=False),
      waits=[("sv", sv_l3)], inc=None)
    E("tensor", lambda e: e.matmul(ps[7][:, 16:17], onesK1[:], bsel_s[:], start=False, stop=False),
      waits=[("ld", ld_bsel)], inc=None)
    E("tensor", lambda e: e.matmul(ps[7][:, 24:32], onesK1[:], ebias1[:], start=False, stop=True),
      waits=[("ld", ld_setup)], inc=("st", 1))
    st_bc = p.c("st")
    E("vector", lambda e: e.tensor_copy(lamb[:], ps[7][:, 0:16]), waits=[("st", st_bc)])
    E("vector", lambda e: e.tensor_copy(bselb[:], ps[7][:, 16:17]))
    E("vector", lambda e: e.tensor_copy(ebias_b[:], ps[7][:, 24:32]), inc=("sv", 1))
    sv_bc = p.c("sv")

    def upd(tr, *new):
        return list(new) + [w for w in tr if w[0] in ("ld", "gd")]

    xt_free = [[], [], [], []]
    xnt_free = [[], []]
    wkv_free = [[], []]
    Xks_free = [[] for _ in range(4)]
    Xkv_free = [[] for _ in range(4)]
    Sfm_free = [[] for _ in range(4)]
    lam_free = [[], []]
    rme_free = [[], []]
    ostage_free = [[]]
    xnfm_ready_grp = {}
    kbank_free = {2: [], 3: []}
    vbank_free = {4: [], 5: []}
    tbank_free = {0: [], 1: []}
    sbank_free = {6: [], 7: []}
    head_tts = {}
    phik_ready = {}
    qst_free = []

    ACT_SET_A = "natural_log_exp_and_others"

    xnfm_free = [[], []]
    ln_ready = {}

    def emit_ln_group(g):
        xf = g % 2
        for i in range(4):
            tile = 4 * g + i
            slot = i
            src = d_xp if tile < 32 else d_xo
            row0 = (tile % 32) * 128
            w_old = list(xt_free[slot])
            E("gpsimd", lambda e, s=slot, src=src, r=row0: e.dma_start(
                out=x_t[s][:], in_=src[r:r + 128, :]), waits=w_old, inc=("gd", 16))
            ldx = p.c("gd")
            for j in range(4):
                E("vector", lambda e, s=slot, i=i, j=j: e.bn_stats(
                    sraw[:, i * 24 + j * 6: i * 24 + j * 6 + 6],
                    x_t[s][:, j * 512:(j + 1) * 512]), waits=[("gd", ldx)] if j == 0 else [],
                  inc=("sv", 1) if j == 3 else None)
            sv_bn = p.c("sv")
            E("vector", lambda e, i=i: e.bn_aggr(stat[:, i * 2:i * 2 + 2],
                                                 sraw[:, i * 24:(i + 1) * 24]),
              waits=[("sv", sv_bn)], inc=("sv", 1))
        sv_aggr = p.c("sv")
        E("scalar", lambda e: e.activation(out=lnv4[:], in_=stat[:].rearrange("p (t k) -> p k t", k=2)[:, 1, :],
                                           func=AF.Ln, bias=eps_ln[:]),
          waits=[("sv", sv_aggr), ("sv", sv_ones)], inc=("ss", 1))
        ss_ln = p.c("ss")
        E("scalar", lambda e: e.activation(out=rstd4[:], in_=lnv4[:], func=AF.Exp, scale=-0.5),
          waits=[("ss", ss_ln)], inc=("ss", 1))
        ss_rstd = p.c("ss")
        E("vector", lambda e: e.scalar_tensor_tensor(
            out=nb4[:], in0=stat[:].rearrange("p (t k) -> p k t", k=2)[:, 0, :],
            scalar=-1.0, in1=rstd4[:], op0=ALU.mult, op1=ALU.mult),
          waits=[("ss", ss_rstd)], inc=("sv", 1))
        sv_nb = p.c("sv")
        for i in range(4):
            tile = 4 * g + i
            slot = tile % 2
            E("scalar", lambda e, s=slot, xs=i, i=i: e.activation(
                out=xn_t[s][:], in_=x_t[xs][:], func=AF.Identity,
                scale=rstd4[:, i:i + 1], bias=nb4[:, i:i + 1]),
              waits=[("sv", sv_nb)] + xnt_free[slot], inc=("ss", 1))
            ss_xn = p.c("ss")
            xt_free[i] = upd(xt_free[i], ("ss", ss_xn))
            if t_xn is not None and tile == 4 * tap_g:
                E("sync", lambda e, s=slot: e.dma_start(out=t_xn, in_=xn_t[s][:]),
                  waits=[("ss", ss_xn)], inc=("ld", 16))
                xnt_free[slot] = xnt_free[slot] + [("ld", p.c("ld"))]
            for half in range(2):
                bank = half
                w = [("ss", ss_xn)] + tbank_free[bank]
                for kc8 in range(8):
                    kc = half * 8 + kc8
                    E("tensor", lambda e, s=slot, kc=kc, b=bank, kc8=kc8: e.transpose(
                        ps_bf[b][:, kc8 * 128:(kc8 + 1) * 128],
                        xn_t[s][:, kc * 128:(kc + 1) * 128], idn_h[:]),
                      waits=w if kc8 == 0 else [],
                      inc=("st", 1) if kc8 == 7 else None)
                st_tp = p.c("st")
                E("vector", lambda e, i=i, half=half, b=bank, xf=xf: e.tensor_copy(
                    xn_fm[xf][:, half * 8:(half + 1) * 8, i * 128:(i + 1) * 128], ps_bf[b][:]),
                  waits=[("st", st_tp)] + (xnfm_free[xf] if (i == 0 and half == 0) else []),
                  inc=("sv", 1))
                tbank_free[bank] = upd(tbank_free[bank], ("sv", p.c("sv")))
            xnt_free[slot] = upd(xnt_free[slot], ("st", st_tp))
        ln_ready[g] = p.c("sv")

    emit_ln_group(0)

    deno_q = []

    def emit_deno(h, sfs, g):
            nonlocal oblk_free
            sv_ready = head_tts[h]
            hp = h % 2
            st_pair = []
            for pi in range(2):
                bank = 6 + pi
                w2 = [("sv", sv_ready)] + sbank_free[bank]
                for bi in range(4):
                    i = 2 * pi + bi // 2
                    half = bi % 2
                    E("tensor", lambda e, sfs=sfs, i=i, half=half, b=bank, bi=bi: e.transpose(
                        ps[b][:, bi * 128:(bi + 1) * 128],
                        S_fm[sfs][:, half * 512 + i * 128: half * 512 + (i + 1) * 128],
                        idn_f[:]),
                      waits=w2 if bi == 0 else [],
                      inc=("st", 1) if bi == 3 else None)
                st_pair.append(p.c("st"))
            Sfm_free[sfs] = upd(Sfm_free[sfs], ("st", st_pair[-1]))
            for i in range(4):
                bank = 6 + i // 2
                ks_off = (2 * (i % 2) + 1) * 128
                qo = (i * 16 + h) * 128
                E("vector", lambda e, qo=qo, b=bank, ko=ks_off, i=i: e.tensor_tensor(
                    out=scr4[:, i * 128:(i + 1) * 128], in0=q_st[:, qo:qo + 128],
                    in1=ps[b][:, ko:ko + 128], op=ALU.mult),
                  waits=([("st", st_pair[0]), ("st", st_pair[1]),
                          ("sv", sg_q_grp)] if i == 0 else []),
                  inc=("sv", 1) if i == 3 else None)
            sv_tt4 = p.c("sv")
            for i in range(4):
                E("vector", lambda e, i=i, hp=hp: e.tensor_reduce(
                    out=den4[:, hp * 4 + i:hp * 4 + i + 1],
                    in_=scr4[:, i * 128:(i + 1) * 128],
                    axis=mybir.AxisListType.X, op=ALU.add),
                  waits=[("sv", sv_tt4)] if i == 0 else [],
                  inc=("sv", 1) if i == 3 else None)
            sv_r4 = p.c("sv")
            E("vector", lambda e, hp=hp: e.tensor_scalar(
                out=den4[:, hp * 4:hp * 4 + 4], in0=den4[:, hp * 4:hp * 4 + 4],
                scalar1=1e-6, scalar2=None, op0=ALU.max),
              waits=[("sv", sv_r4)], inc=("sv", 1))
            sv_c4 = p.c("sv")
            E("vector", lambda e, hp=hp: e.reciprocal(
                dinv4[:, hp * 4:hp * 4 + 4], den4[:, hp * 4:hp * 4 + 4]),
              waits=[("sv", sv_c4)], inc=("sv", 1))
            sv_i4 = p.c("sv")
            if t_den is not None and g == tap_g and h == 3:
                E("vector", lambda e, hp=hp: e.tensor_copy(tden_b[:], dinv4[:, hp * 4:hp * 4 + 4]),
                  waits=[("sv", sv_i4)], inc=("sv", 1))
            for i in range(4):
                bank = 6 + i // 2
                kv_off = (2 * (i % 2)) * 128
                qo = (i * 16 + h) * 128
                E("vector", lambda e, qo=qo, b=bank, ko=kv_off, i=i, hp=hp: e.scalar_tensor_tensor(
                    out=o_blk4[:, i * 128:(i + 1) * 128], in0=q_st[:, qo:qo + 128],
                    scalar=dinv4[:, hp * 4 + i:hp * 4 + i + 1],
                    in1=ps[b][:, ko:ko + 128], op0=ALU.mult, op1=ALU.mult),
                  waits=([("sv", sv_i4)] + oblk_free) if i == 0 else [],
                  inc=("sv", 1) if i == 3 else None)
            sv_o4 = p.c("sv")
            sbank_free[6] = upd(sbank_free[6], ("sv", sv_o4))
            sbank_free[7] = upd(sbank_free[7], ("sv", sv_o4))
            obank = h % 2
            w3 = [("sv", sv_o4)] + tbank_free[obank]
            for i in range(4):
                E("tensor", lambda e, i=i, b=obank: e.transpose(
                    ps_bf[b][:, i * 128:(i + 1) * 128],
                    o_blk4[:, i * 128:(i + 1) * 128], idn_h[:]),
                  waits=w3 if i == 0 else [],
                  inc=("st", 1) if i == 3 else None)
            st_ot = p.c("st")
            oblk_free = upd(oblk_free, ("st", st_ot))
            E("vector", lambda e, h=h, b=obank: e.tensor_copy(
                o_stg[:, :, h, :], ps_bf[b][:, 0:512].rearrange(
                    "p (a c) -> p a c", a=4)),
              waits=[("st", st_ot)] + ostage_free[0], inc=("sv", 1))
            tbank_free[obank] = upd(tbank_free[obank], ("sv", p.c("sv")))


    for g in range(n_groups):
        own = g >= 8
        xf = g % 2
        sv_xnfm = ln_ready[g]
        piece_seq = []
        if own:
            for j in range(4):
                piece_seq.append(("q", j))
        for j in range(4):
            piece_seq.append(("k", 4 + j))
            piece_seq.append(("v", 8 + j))

        if g == 1:
            for pc in (0, 1, 2, 3):
                E("gpsimd", lambda e, pc=pc: e.dma_start(out=d_wqkv_h[pc], in_=d_wqkv[pc]),
                  inc=("gd", 16))
                cast_done[("qkv", pc)] = p.c("gd")
        if g == 2 and do_phase_b:
            for nm, dst, src2 in (("wout", d_wout_h, d_wout), ("wgad", d_wgad_h, d_wgad),
                                  ("wuad", d_wuad_h, d_wuad), ("wdn", d_wdn_h, d_wdn)):
                E("gpsimd", lambda e, dst=dst, src2=src2: e.dma_start(out=dst[:], in_=src2[:]),
                  inc=("gd", 16))
                cast_done[nm] = p.c("gd")
            for dst, src2 in ((acat_g_s, d_acat_g), (acat_u_s, d_acat_u),
                              (bcat_d_s, d_bcat_d), (wrout_s, d_wrout), (acat_d_s, d_acat_d)):
                E("gpsimd", lambda e, dst=dst, src2=src2: e.dma_start(out=dst[:], in_=src2[:]),
                  inc=("gd", 16))
            E("gpsimd", lambda e: e.dma_start(out=s8_s[:], in_=d_s8), inc=("gd", 16))
            gd_smallw = p.c("gd")
        if g + 1 < n_groups:
            emit_ln_group(g + 1)
        wslot = 0
        sg_q_grp = p.c("sv")
        oblk_free = []
        for kind, pidx in piece_seq:
            wslot ^= 1
            if g < 2:
                E("gpsimd", lambda e, s=wslot, pc=pidx: e.dma_start(
                    out=wkv[s][:], in_=d_wqkv[pc]),
                  waits=list(wkv_free[wslot]), inc=("gd", 16))
                ldw_sem, ldw = "gd", p.c("gd")
                if g == 0:
                    E("gpsimd", lambda e, pc=pidx: e.dma_start(out=d_wqkv_h[pc], in_=d_wqkv[pc]),
                      inc=("gd", 16))
                    cast_done[("qkv", pidx)] = p.c("gd")
                elif g == 1 and pidx in (0, 1, 2, 3) and ("qkv", pidx) not in cast_done:
                    E("gpsimd", lambda e, pc=pidx: e.dma_start(out=d_wqkv_h[pc], in_=d_wqkv[pc]),
                      inc=("gd", 16))
                    cast_done[("qkv", pidx)] = p.c("gd")
            else:
                if pidx % 2 == 1:
                    E("gpsimd", lambda e, s=wslot, pc=pidx: e.dma_start(
                        out=wkv[s][:], in_=d_wqkv_h[pc]),
                      waits=[("gd", cast_done[("qkv", pidx)])] + wkv_free[wslot],
                      inc=("gd", 16))
                    ldw_sem, ldw = "gd", p.c("gd")
                else:
                    E("sync", lambda e, s=wslot, pc=pidx: e.dma_start(
                        out=wkv[s][:], in_=d_wqkv_h[pc]),
                      waits=[("gd", cast_done[("qkv", pidx)])] + wkv_free[wslot],
                      inc=("ld", 16))
                    ldw_sem, ldw = "ld", p.c("ld")
            if kind == "k":
                for s4 in range(4):
                    h = (pidx - 4) * 4 + s4
                    bank = 2 + (h % 2)
                    rme = h % 2
                    w = [(ldw_sem, ldw), ("sv", sv_xnfm)] + kbank_free[bank]
                    for kc in range(16):
                        E("tensor", lambda e, s=wslot, kc=kc, s4=s4, b=bank, xf=xf: e.matmul(
                            ps[b][:], wkv[s][:, kc, s4 * 128:(s4 + 1) * 128],
                            xn_fm[xf][:, kc, :], start=(kc == 0), stop=(kc == 15)),
                          waits=w if kc == 0 else [], inc=("st", 1) if kc == 15 else None)
                    st_k = p.c("st")
                    E("scalar", lambda e, b=bank, r=rme: e.activation(
                        out=m_buf[r][:], in_=ps[b][:], func=AF.Relu, scale=-1.0),
                      waits=[("st", st_k)] + rme_free[rme], inc=None)
                    E("scalar", lambda e, b=bank, r=rme: e.activation(
                        out=r_buf[r][:], in_=ps[b][:], func=AF.Relu),
                      inc=("ss", 1))
                    ss_r = p.c("ss")
                    kbank_free[bank] = upd(kbank_free[bank], ("ss", ss_r))
                    E("scalar", lambda e, r=rme: e.activation(
                        out=e_buf[r][:], in_=m_buf[r][:], func=AF.Exp, scale=-1.0),
                      waits=[("ss", ss_r)], inc=("ss", 1))
                    ss_e = p.c("ss")
                    hs = h % 4
                    E("vector", lambda e, r=rme, hs=hs: e.tensor_add(
                        X_ks[hs][:], e_buf[r][:], r_buf[r][:]),
                      waits=[("ss", ss_e)] + Xks_free[hs], inc=("sv", 1))
                    sv_phik = p.c("sv")
                    phik_ready[h] = sv_phik
                    rme_free[rme] = upd(rme_free[rme], ("sv", sv_phik))
                    if t_phik is not None and g == tap_g and h == 3:
                        E("gpsimd", lambda e, hs=hs: e.dma_start(out=t_phik, in_=X_ks[hs][:]),
                          waits=[("sv", sv_phik)], inc=("gd", 16))
                        Xks_free[hs] = Xks_free[hs] + [("gd", p.c("gd"))]
                wkv_free[wslot] = upd(wkv_free[wslot], ("st", p.c("st")))
            elif kind == "v":
                for s4 in range(4):
                    h = (pidx - 8) * 4 + s4
                    hs = h % 4
                    bank = 4 + (h % 2)
                    w = [(ldw_sem, ldw), ("sv", sv_xnfm)] + vbank_free[bank]
                    for kc in range(16):
                        E("tensor", lambda e, s=wslot, kc=kc, s4=s4, b=bank, xf=xf: e.matmul(
                            ps[b][:], wkv[s][:, kc, s4 * 128:(s4 + 1) * 128],
                            xn_fm[xf][:, kc, :], start=(kc == 0), stop=(kc == 15)),
                          waits=w if kc == 0 else [], inc=("st", 1) if kc == 15 else None)
                    st_v = p.c("st")
                    E("vector", lambda e, b=bank, hs=hs: e.tensor_tensor(
                        out=X_kv[hs][:], in0=X_ks[hs][:], in1=ps[b][:], op=ALU.mult),
                      waits=[("st", st_v), ("sv", phik_ready[h])] + Xkv_free[hs],
                      inc=("sv", 1))
                    sv_kv = p.c("sv")
                    vbank_free[bank] = upd(vbank_free[bank], ("sv", sv_kv))
                    if t_kv is not None and g == tap_g and h == 3:
                        E("sync", lambda e, hs=hs: e.dma_start(out=t_kv, in_=X_kv[hs][:]),
                          waits=[("sv", sv_kv)], inc=("ld", 16))
                        Xkv_free[hs] = Xkv_free[hs] + [("ld", p.c("ld"))]
                    hslot = h % 2
                    sfs = h % 4
                    if g == 8:
                        E("vector", lambda e, hl=hslot, h=h: e.tensor_scalar(
                            out=lam_t[hl][:], in0=ones512[:], scalar1=lamb[:, h:h + 1],
                            scalar2=None, op0=ALU.mult),
                          waits=lam_free[hslot], inc=("sv", 1))
                        sv_lam = p.c("sv")
                        E("vector", lambda e, hl=hslot: e.tensor_tensor(
                            out=lam_t[hl][:, 0:1], in0=lam_t[hl][:, 0:1], in1=bselb[:],
                            op=ALU.mult),
                          waits=[("sv", sv_lam)], inc=("sv", 1))
                        sv_lam = p.c("sv")
                        data0 = lam_t[hslot][:]
                        w_tts = [("sv", sv_lam)] + Sfm_free[sfs]
                    else:
                        data0 = lamb[:, h:h + 1].to_broadcast((128, 512))
                        w_tts = ([("sv", sv_bc)] if g == 0 else []) + list(Sfm_free[sfs])
                    if g > 0:
                        w_tts.append(("sv", head_tts[h]))
                    if g == 0:
                        E("vector", lambda e, d0=data0, hs=hs, sfs=sfs: e.tensor_tensor_scan(
                            out=S_fm[sfs][:, 0:512], data0=d0, data1=X_kv[hs][:],
                            initial=0.0, op0=ALU.mult, op1=ALU.add), waits=w_tts)
                        E("vector", lambda e, d0=data0, hs=hs, sfs=sfs: e.tensor_tensor_scan(
                            out=S_fm[sfs][:, 512:1024], data0=d0, data1=X_ks[hs][:],
                            initial=0.0, op0=ALU.mult, op1=ALU.add), inc=("sv", 1))
                    else:
                        E("vector", lambda e, d0=data0, hs=hs, sfs=sfs, h=h: e.tensor_tensor_scan(
                            out=S_fm[sfs][:, 0:512], data0=d0, data1=X_kv[hs][:],
                            initial=carry[:, 2 * h:2 * h + 1], op0=ALU.mult, op1=ALU.add),
                          waits=w_tts)
                        E("vector", lambda e, d0=data0, hs=hs, sfs=sfs, h=h: e.tensor_tensor_scan(
                            out=S_fm[sfs][:, 512:1024], data0=d0, data1=X_ks[hs][:],
                            initial=carry[:, 2 * h + 1:2 * h + 2], op0=ALU.mult, op1=ALU.add),
                          inc=("sv", 1))
                    sv_tts = p.c("sv")
                    if g == 8:
                        lam_free[hslot] = upd(lam_free[hslot], ("sv", sv_tts))
                    Xkv_free[hs] = upd(Xkv_free[hs], ("sv", sv_tts))
                    Xks_free[hs] = upd(Xks_free[hs], ("sv", sv_tts))
                    E("vector", lambda e, sfs=sfs, h=h: e.tensor_copy(
                        carry[:, 2 * h:2 * h + 2],
                        S_fm[sfs][:].rearrange("p (a b) -> p a b", a=2)[:, :, 511]),
                      waits=[("sv", sv_tts)], inc=("sv", 1))
                    head_tts[h] = p.c("sv")
                    if t_S is not None and g == tap_g and h == 3:
                        E("sync", lambda e, sfs=sfs: e.dma_start(out=t_S, in_=S_fm[sfs][:]),
                          waits=[("sv", head_tts[h])], inc=("ld", 16))
                        Sfm_free[sfs] = Sfm_free[sfs] + [("ld", p.c("ld"))]
                    if not own:
                        Sfm_free[sfs] = upd(Sfm_free[sfs], ("sv", head_tts[h]))
                    else:
                        deno_q.append((h, sfs, g))
                        if len(deno_q) > 2:
                            emit_deno(*deno_q.pop(0))
                wkv_free[wslot] = upd(wkv_free[wslot], ("st", p.c("st")))
            else:
                pq = pidx
                for i in range(4):
                    bank = 4 + (i % 2)
                    rme = i % 2
                    w = [(ldw_sem, ldw), ("sv", sv_xnfm)] + vbank_free[bank] + qst_free
                    for kc in range(16):
                        E("tensor", lambda e, s=wslot, kc=kc, i=i, b=bank, xf=xf: e.matmul(
                            ps[b][:], xn_fm[xf][:, kc, i * 128:(i + 1) * 128],
                            wkv[s][:, kc, :], start=(kc == 0), stop=(kc == 15)),
                          waits=w if kc == 0 else [], inc=("st", 1) if kc == 15 else None)
                    st_q = p.c("st")
                    E("scalar", lambda e, b=bank, r=rme: e.activation(
                        out=m_buf[r][:], in_=ps[b][:], func=AF.Relu, scale=-1.0),
                      waits=[("st", st_q)] + rme_free[rme], inc=None)
                    E("scalar", lambda e, b=bank, r=rme: e.activation(
                        out=r_buf[r][:], in_=ps[b][:], func=AF.Relu),
                      inc=("ss", 1))
                    ss_r = p.c("ss")
                    vbank_free[bank] = upd(vbank_free[bank], ("ss", ss_r))
                    E("scalar", lambda e, r=rme: e.activation(
                        out=e_buf[r][:], in_=m_buf[r][:], func=AF.Exp, scale=-1.0),
                      waits=[("ss", ss_r)], inc=("ss", 1))
                    ss_e = p.c("ss")
                    qoff = (i * 16 + 4 * pq) * 128
                    E("vector", lambda e, r=rme, qo=qoff: e.tensor_add(
                        q_st[:, qo:qo + 512], e_buf[r][:], r_buf[r][:]),
                      waits=[("ss", ss_e)] + qst_free, inc=("sv", 1))
                    rme_free[rme] = upd(rme_free[rme], ("sv", p.c("sv")))
                wkv_free[wslot] = upd(wkv_free[wslot], ("st", p.c("st")))
                sg_q_grp = p.c("sv")

        while deno_q:
            emit_deno(*deno_q.pop(0))
        xnfm_free[xf] = [("st", p.c("st"))]
        if own:
            if t_q is not None and g == tap_g:
                E("sync", lambda e: e.dma_start(out=t_q, in_=q_st[:, 0:2048]),
                  waits=[("sv", sg_q_grp)], inc=("ld", 16))
                qst_free = qst_free + [("ld", p.c("ld"))]
            qst_free = upd(qst_free, ("sv", p.c("sv")))
            if t_den is not None and g == tap_g:
                E("sync", lambda e: e.dma_start(out=t_den, in_=tden_b[:]),
                  waits=[("sv", p.c("sv"))], inc=("ld", 16))
            E("gpsimd", lambda e, g=g: e.dma_start(
                out=d_ospill[:, (g - 8) * 4:(g - 8) * 4 + 4, :, :], in_=o_stg[:]),
              waits=[("sv", p.c("sv"))], inc=("gd", 16))
            ostage_free[0] = upd(ostage_free[0], ("gd", p.c("gd")))
            if t_o is not None and g == tap_g:
                E("gpsimd", lambda e: e.dma_start(out=t_o, in_=o_stg[:, 0, :, :]),
                  inc=("gd", 16))

    if not do_phase_b:
        E("sync", lambda e: e.nop(), waits=[("sv", p.c("sv")), ("ss", p.c("ss")),
                                                   ("st", p.c("st")), ("sg", p.c("sg")),
                                                   ("gd", p.c("gd")), ("ld", p.c("ld"))])
        _emit(nc, p)
        return nc, p, tap_aps

    bar = [("sv", p.c("sv")), ("ss", p.c("ss")), ("st", p.c("st")),
           ("sg", p.c("sg")), ("gd", p.c("gd")), ("ld", p.c("ld"))]
    for eng in ("sync", "gpsimd", "tensor", "vector", "scalar"):
        E(eng, lambda e: e.nop(), waits=bar)
    stA.close()

    xb = [sb(f"xb{j}", [128, DM], F32) for j in range(4)]
    o_fm = sb("o_fm", [128, 4, 16, 128], BF16)
    xn2_t = sb("xn2_t", [128, DM], BF16)
    xn2_fm = sb("xn2_fm", [128, 16, 512], BF16)
    h_buf = sb("h_buf", [128, NSUB, 512], BF16)
    wout_b = [sb(f"wout_b{i}", [128, 16, 512], BF16) for i in range(2)]
    wga = [sb(f"wga{i}", [128, 2176], BF16) for i in range(2)]
    wua = [sb(f"wua{i}", [128, 2176], BF16) for i in range(2)]
    wdn_b = [sb(f"wdn_b{i}", [128, 11, 512], BF16) for i in range(2)]
    sg_b = sb("sg_b", [128, 512], F32)
    Pg = sg_b[0:R9, :]
    Pgs = sb("Pgs", [R9, 512], BF16)
    Pus = sb("Pus", [R9, 512], BF16)
    Pds = sb("Pds", [R9, 512], BF16)
    w_em = sb("w_em", [8, 512], BF16)
    w_exp = sb("w_exp", [64, 512], BF16)
    sraw2, stat2, lnv4b, rstd4b, nb4b = sraw, stat, lnv4, rstd4, nb4
    sc8 = sb("sc8", [128, 8], F32)
    bi8 = sb("bi8", [128, 8], F32)
    mk8 = sb("mk8", [128, 8], F32)
    tm8 = sb("tm8", [128, 8], F32)
    m1b = sb("m1b", [128, 1], F32)
    m2b = sb("m2b", [128, 1], F32)
    wsum = sb("wsum", [128, 1], F32)
    wtok = sb("wtok", [128, 8], BF16)

    xb_free = [[] for _ in range(4)]
    ofm_free = []
    woutb_free = [[], []]
    wga_free = [[], []]
    wua_free = [[], []]
    wdnb_free = [[], []]
    xn2fm_free = []
    hbuf_free = []
    pfree = {i: [] for i in range(8)}

    ld_o = None
    ldx_b = None

    def emit_b_loads(b):
        nonlocal ld_o, ldx_b
        tiles = [4 * b + j for j in range(4)]
        E("sync", lambda e, b=b: e.dma_start(
            out=o_fm[:], in_=d_ospill[:, 4 * b:4 * b + 4, :, :]),
          waits=list(ofm_free), inc=("ld", 16))
        ld_o = p.c("ld")
        ldx_b = []
        for j in range(4):
            E("sync", lambda e, j=j, t=tiles[j]: e.dma_start(
                out=xb[j][:], in_=d_xo[t * 128:(t + 1) * 128, :]),
              waits=list(xb_free[j]), inc=("ld", 16))
            ldx_b.append(p.c("ld"))

    emit_b_loads(0)
    for b in range(NB):
        tiles = [4 * b + j for j in range(4)]
        ld_o_b, ldx_b_b = ld_o, ldx_b
        sv_x1 = []
        for fo in range(4):
            ws = fo % 2
            E("sync", lambda e, ws=ws, fo=fo: e.dma_start(
                out=wout_b[ws][:], in_=d_wout_h[fo]),
              waits=[("gd", cast_done["wout"])] + woutb_free[ws], inc=("ld", 16))
            ld_w = p.c("ld")
            for j in range(4):
                bank = 4 + (j % 2)
                w = [("ld", ld_w), ("ld", ld_o_b), ("ld", ldx_b_b[j])] + pfree[bank]
                for kc in range(16):
                    E("tensor", lambda e, j=j, kc=kc, ws=ws, bk=bank: e.matmul(
                        ps[bk][:], o_fm[:, j, kc, :], wout_b[ws][:, kc, :],
                        start=(kc == 0), stop=(kc == 15)),
                      waits=w if kc == 0 else [], inc=("st", 1) if kc == 15 else None)
                st_w = p.c("st")
                E("vector", lambda e, j=j, fo=fo, bk=bank: e.tensor_tensor(
                    out=xb[j][:, fo * 512:(fo + 1) * 512], in0=ps[bk][:],
                    in1=xb[j][:, fo * 512:(fo + 1) * 512], op=ALU.add),
                  waits=[("st", st_w)], inc=("sv", 1))
                pfree[bank] = upd(pfree[bank], ("sv", p.c("sv")))
                sv_x1.append(p.c("sv"))
            woutb_free[ws] = upd(woutb_free[ws], ("st", p.c("st")))
        sv_x1_all = p.c("sv")
        if t_x1 is not None and b == 0:
            for j in range(4):
                E("sync", lambda e, j=j: e.dma_start(out=t_x1[:, j * DM:(j + 1) * DM],
                                                     in_=xb[j][:]),
                  waits=[("sv", sv_x1_all)], inc=("ld", 16))
                xb_free[j] = xb_free[j] + [("ld", p.c("ld"))]
        for j in range(4):
            for q4 in range(4):
                E("vector", lambda e, j=j, q4=q4: e.bn_stats(
                    sraw2[:, j * 24 + q4 * 6: j * 24 + q4 * 6 + 6],
                    xb[j][:, q4 * 512:(q4 + 1) * 512]),
                  waits=[("sv", sv_x1_all)] if (j == 0 and q4 == 0) else [],
                  inc=("sv", 1) if q4 == 3 else None)
            sv_bn = p.c("sv")
            E("vector", lambda e, j=j: e.bn_aggr(stat2[:, j * 2:j * 2 + 2],
                                                 sraw2[:, j * 24:(j + 1) * 24]),
              waits=[("sv", sv_bn)], inc=("sv", 1))
        sv_aggr = p.c("sv")
        E("scalar", lambda e: e.activation(
            out=lnv4b[:], in_=stat2[:].rearrange("p (t k) -> p k t", k=2)[:, 1, :],
            func=AF.Ln, bias=eps_ln[:]),
          waits=[("sv", sv_aggr)], inc=("ss", 1))
        ss_ln = p.c("ss")
        E("scalar", lambda e: e.activation(out=rstd4b[:], in_=lnv4b[:], func=AF.Exp,
                                           scale=-0.5),
          waits=[("ss", ss_ln)], inc=("ss", 1))
        ss_rstd = p.c("ss")
        E("vector", lambda e: e.scalar_tensor_tensor(
            out=nb4b[:], in0=stat2[:].rearrange("p (t k) -> p k t", k=2)[:, 0, :],
            scalar=-1.0, in1=rstd4b[:], op0=ALU.mult, op1=ALU.mult),
          waits=[("ss", ss_rstd)], inc=("sv", 1))
        sv_nb = p.c("sv")
        sv_rout = []
        for j in range(4):
            E("scalar", lambda e, j=j: e.activation(
                out=xn2_t[:], in_=xb[j][:], func=AF.Identity,
                scale=rstd4b[:, j:j + 1], bias=nb4b[:, j:j + 1]),
              waits=[("sv", sv_nb), ("st", p.c("st"))], inc=("ss", 1))
            ss_xn2 = p.c("ss")
            if t_xn2 is not None and b == 0 and j == 0:
                E("sync", lambda e: e.dma_start(out=t_xn2, in_=xn2_t[:]),
                  waits=[("ss", ss_xn2)], inc=("ld", 16))
            for half in range(2):
                bank = half
                w = [("ss", ss_xn2)] + pfree[bank]
                for kc8 in range(8):
                    kc = half * 8 + kc8
                    E("tensor", lambda e, kc=kc, bk=bank, kc8=kc8: e.transpose(
                        ps_bf[bk][:, kc8 * 128:(kc8 + 1) * 128],
                        xn2_t[:, kc * 128:(kc + 1) * 128], idn_h[:]),
                      waits=w if kc8 == 0 else [],
                      inc=("st", 1) if kc8 == 7 else None)
                st_tp = p.c("st")
                E("vector", lambda e, j=j, half=half, bk=bank: e.tensor_copy(
                    xn2_fm[:, half * 8:(half + 1) * 8, j * 128:(j + 1) * 128],
                    ps_bf[bk][:]),
                  waits=[("st", st_tp)] + xn2fm_free, inc=("sv", 1))
                pfree[bank] = upd(pfree[bank], ("sv", p.c("sv")))
        sv_xn2fm = p.c("sv")
        xn2fm_free = []
        for j in range(4):
            w = [("sv", sv_xn2fm), ("gd", gd_smallw)] + pfree[7]
            for kc in range(16):
                E("tensor", lambda e, j=j, kc=kc: e.matmul(
                    ps[7][:, 0:8], xn2_fm[:, kc, j * 128:(j + 1) * 128],
                    wrout_s[:, kc, :], start=(kc == 0), stop=(kc == 15)),
                  waits=w if kc == 0 else [], inc=("st", 1) if kc == 15 else None)
            st_r = p.c("st")
            E("scalar", lambda e: e.activation(out=sc8[:], in_=ps[7][:, 0:8],
                                               func=AF.Exp, scale=-1.0),
              waits=[("st", st_r)], inc=("ss", 1))
            ss_e8 = p.c("ss")
            pfree[7] = upd(pfree[7], ("ss", ss_e8))
            E("vector", lambda e: e.tensor_scalar(out=sc8[:], in0=sc8[:], scalar1=1.0,
                                                  scalar2=None, op0=ALU.add),
              waits=[("ss", ss_e8)], inc=("sv", 1))
            E("vector", lambda e: e.reciprocal(sc8[:], sc8[:]),
              waits=[("sv", p.c("sv"))], inc=("sv", 1))
            E("vector", lambda e: e.tensor_add(bi8[:], sc8[:], ebias_b[:]),
              waits=[("sv", p.c("sv"))], inc=("sv", 1))
            E("vector", lambda e: e.tensor_reduce(out=m1b[:], in_=bi8[:],
                                                  axis=mybir.AxisListType.X, op=ALU.max),
              waits=[("sv", p.c("sv"))], inc=("sv", 1))
            E("vector", lambda e: e.tensor_scalar(out=mk8[:], in0=bi8[:], scalar1=m1b[:],
                                                  scalar2=None, op0=ALU.is_ge),
              waits=[("sv", p.c("sv"))], inc=("sv", 1))
            E("vector", lambda e: e.scalar_tensor_tensor(
                out=tm8[:], in0=mk8[:], scalar=-1e30, in1=bi8[:],
                op0=ALU.mult, op1=ALU.add),
              waits=[("sv", p.c("sv"))], inc=("sv", 1))
            E("vector", lambda e: e.tensor_reduce(out=m2b[:], in_=tm8[:],
                                                  axis=mybir.AxisListType.X, op=ALU.max),
              waits=[("sv", p.c("sv"))], inc=("sv", 1))
            E("vector", lambda e: e.tensor_scalar(out=mk8[:], in0=bi8[:], scalar1=m2b[:],
                                                  scalar2=None, op0=ALU.is_ge),
              waits=[("sv", p.c("sv"))], inc=("sv", 1))
            E("vector", lambda e: e.tensor_tensor(out=mk8[:], in0=sc8[:], in1=mk8[:],
                                                  op=ALU.mult),
              waits=[("sv", p.c("sv"))], inc=("sv", 1))
            E("vector", lambda e: e.tensor_reduce(out=wsum[:], in_=mk8[:],
                                                  axis=mybir.AxisListType.X, op=ALU.add),
              waits=[("sv", p.c("sv"))], inc=("sv", 1))
            E("vector", lambda e: e.tensor_scalar(out=wsum[:], in0=wsum[:], scalar1=1e-8,
                                                  scalar2=None, op0=ALU.add),
              waits=[("sv", p.c("sv"))], inc=("sv", 1))
            E("vector", lambda e: e.reciprocal(wsum[:], wsum[:]),
              waits=[("sv", p.c("sv"))], inc=("sv", 1))
            E("vector", lambda e: e.tensor_scalar(out=wtok[:], in0=mk8[:], scalar1=wsum[:],
                                                  scalar2=None, op0=ALU.mult),
              waits=[("sv", p.c("sv"))], inc=("sv", 1))
            sv_wt = p.c("sv")
            if t_rout is not None and b == 0:
                E("gpsimd", lambda e, j=j: e.dma_start(out=t_rout[:, j * 8:(j + 1) * 8],
                                                       in_=wtok[:]),
                  waits=[("sv", sv_wt)], inc=("gd", 16))
            E("tensor", lambda e: e.transpose(ps_bf[7][0:8, 0:128], wtok[:], idn_h[:]),
              waits=[("sv", sv_wt)] + pfree[7], inc=("st", 1))
            st_wt = p.c("st")
            E("vector", lambda e, j=j: e.tensor_copy(w_em[:, j * 128:(j + 1) * 128],
                                                     ps_bf[7][0:8, 0:128]),
              waits=[("st", st_wt)], inc=("sv", 1))
            pfree[7] = upd(pfree[7], ("sv", p.c("sv")))
            sv_rout.append(p.c("sv"))
        sv_wem = p.c("sv")
        E("tensor", lambda e: e.matmul(ps[6][0:64, :], s8_s[:], w_em[:],
                                       start=True, stop=True),
          waits=[("sv", sv_wem), ("gd", gd_smallw)] + pfree[6], inc=("st", 1))
        st_we = p.c("st")
        E("vector", lambda e: e.tensor_copy(w_exp[:], ps[6][0:64, :]),
          waits=[("st", st_we)], inc=("sv", 1))
        sv_wexp = p.c("sv")
        pfree[6] = upd(pfree[6], ("sv", sv_wexp))
        for (acat, Pst, nm) in ((acat_g_s, Pgs, "g"), (acat_u_s, Pus, "u")):
            w = [("sv", sv_xn2fm)] + pfree[6]
            for kc in range(16):
                E("tensor", lambda e, kc=kc, acat=acat: e.matmul(
                    ps[6][0:R9, :], acat[:, kc, :], xn2_fm[:, kc, :],
                    start=(kc == 0), stop=(kc == 15)),
                  waits=w if kc == 0 else [], inc=("st", 1) if kc == 15 else None)
            st_p = p.c("st")
            E("scalar", lambda e: e.activation(out=Pg[:], in_=ps[6][0:R9, :],
                                               func=AF.Identity, scale=0.125),
              waits=[("st", st_p), ("sv", p.c("sv"))], inc=("ss", 1))
            ss_p = p.c("ss")
            pfree[6] = upd(pfree[6], ("ss", ss_p))
            E("vector", lambda e, Pst=Pst: e.tensor_tensor(
                out=Pst[0:64, :], in0=Pg[0:64, :], in1=w_exp[:], op=ALU.mult),
              waits=[("ss", ss_p), ("sv", sv_wexp)], inc=("sv", 1))
            E("vector", lambda e, Pst=Pst: e.tensor_copy(Pst[64:R9, :], Pg[64:R9, :]),
              inc=("sv", 1))
        sv_pl = p.c("sv")
        sv_h_last = 0
        pd_pending = []
        for s in range(NSUB):
            wsl = s % 2
            E("sync", lambda e, s=s, w=wsl: e.dma_start(out=wga[w][:], in_=d_wgad_h[s]),
              waits=[("gd", cast_done["wgad"])] + wga_free[wsl], inc=("ld", 16))
            ld_g = p.c("ld")
            E("sync", lambda e, s=s, w=wsl: e.dma_start(out=wua[w][:], in_=d_wuad_h[s]),
              waits=[("gd", cast_done["wuad"])] + wua_free[wsl], inc=("ld", 16))
            ld_u = p.c("ld")
            w = [("ld", ld_g), ("sv", sv_pl)] + pfree[2]
            for kc in range(16):
                E("tensor", lambda e, kc=kc, w2=wsl: e.matmul(
                    ps[2][:], wga[w2][:, kc * 128:(kc + 1) * 128], xn2_fm[:, kc, :],
                    start=(kc == 0), stop=False),
                  waits=w if kc == 0 else [])
            E("tensor", lambda e, w2=wsl: e.matmul(
                ps[2][:], wga[w2][0:R9, 2048:2176], Pgs[:],
                start=False, stop=True), inc=("st", 1))
            st_g = p.c("st")
            wga_free[wsl] = upd(wga_free[wsl], ("st", st_g))
            E("scalar", lambda e: e.activation(out=sg_b[:], in_=ps[2][:], func=AF.Silu),
              waits=[("st", st_g)], inc=("ss", 1))
            ss_sg = p.c("ss")
            pfree[2] = upd(pfree[2], ("ss", ss_sg))
            if t_gate is not None and b == 0 and s == 0:
                E("sync", lambda e: e.dma_start(out=t_gate, in_=sg_b[:]),
                  waits=[("ss", ss_sg)], inc=("ld", 16))
            w = [("ld", ld_u), ("sv", sv_pl)] + pfree[3]
            for kc in range(16):
                E("tensor", lambda e, kc=kc, w2=wsl: e.matmul(
                    ps[3][:], wua[w2][:, kc * 128:(kc + 1) * 128], xn2_fm[:, kc, :],
                    start=(kc == 0), stop=False),
                  waits=w if kc == 0 else [])
            E("tensor", lambda e, w2=wsl: e.matmul(
                ps[3][:], wua[w2][0:R9, 2048:2176], Pus[:],
                start=False, stop=True), inc=("st", 1))
            st_u = p.c("st")
            E("vector", lambda e, s=s: e.tensor_tensor(
                out=h_buf[:, s, :], in0=sg_b[:], in1=ps[3][:], op=ALU.mult),
              waits=[("st", st_u), ("ss", ss_sg)] + (hbuf_free if s == 0 else []),
              inc=("sv", 1))
            sv_h = p.c("sv")
            pfree[3] = upd(pfree[3], ("sv", sv_h))
            sv_h_last = sv_h
            pd_pending.append((s, wsl, sv_h))
            if len(pd_pending) > 1 or s == NSUB - 1:
                todo = pd_pending if s == NSUB - 1 else [pd_pending.pop(0)]
                for (s0, w0, svh0) in todo:
                    E("tensor", lambda e, s0=s0: e.matmul(
                        ps[6][0:R9, :], acat_d_s[:, s0, :], h_buf[:, s0, :],
                        start=(s0 == 0), stop=(s0 == NSUB - 1)),
                      waits=[("sv", svh0)] + (pfree[6] if s0 == 0 else []),
                      inc=("st", 1))
                if s == NSUB - 1:
                    pd_pending.clear()
            wua_free[wsl] = upd(wua_free[wsl], ("st", p.c("st")))
        st_pd = p.c("st")
        if t_h is not None and b == 0:
            E("sync", lambda e: e.dma_start(out=t_h, in_=h_buf[:].rearrange("p a b -> p (a b)")),
              waits=[("sv", sv_h_last)], inc=("ld", 16))
            hbuf_free = hbuf_free + [("ld", p.c("ld"))]
        E("scalar", lambda e: e.activation(out=Pg[:], in_=ps[6][0:R9, :],
                                           func=AF.Identity, scale=0.125),
          waits=[("st", st_pd)], inc=("ss", 1))
        ss_pd = p.c("ss")
        pfree[6] = upd(pfree[6], ("ss", ss_pd))
        E("vector", lambda e: e.tensor_tensor(out=Pds[0:64, :], in0=Pg[0:64, :],
                                              in1=w_exp[:], op=ALU.mult),
          waits=[("ss", ss_pd)], inc=("sv", 1))
        E("vector", lambda e: e.tensor_copy(Pds[64:R9, :], Pg[64:R9, :]), inc=("sv", 1))
        sv_pds = p.c("sv")
        kq_ranges = [(0, 11), (11, 22), (22, 33), (33, NSUB)]
        for fo in range(4):
            bankset = [2, 3, 4, 5] if fo % 2 == 0 else [6, 7, 0, 1]
            for kq, (a0, a1) in enumerate(kq_ranges):
                wsl = kq % 2
                E("sync", lambda e, fo=fo, a0=a0, a1=a1, w=wsl: e.dma_start(
                    out=wdn_b[w][:, 0:a1 - a0, :], in_=d_wdn_h[fo][:, a0:a1, :]),
                  waits=[("gd", cast_done["wdn"])] + wdnb_free[wsl], inc=("ld", 16))
                ld_d = p.c("ld")
                for j in range(4):
                    bank = bankset[j]
                    w = [("ld", ld_d), ("sv", sv_h_last)]
                    if kq == 0:
                        w += pfree[bank]
                    nk = a1 - a0
                    for ki, kc in enumerate(range(a0, a1)):
                        E("tensor", lambda e, kc=kc, ki=ki, j=j, bk=bank, w2=wsl, fo=fo: e.matmul(
                            ps[bk][:], h_buf[:, kc, j * 128:(j + 1) * 128],
                            wdn_b[w2][:, ki, :],
                            start=(kc == 0), stop=False),
                          waits=w if ki == 0 else [],
                          inc=("st", 1) if ki == nk - 1 else None)
                    if kq == 3:
                        E("tensor", lambda e, j=j, bk=bank, fo=fo: e.matmul(
                            ps[bk][:], Pds[:, j * 128:(j + 1) * 128], bcat_d_s[:, fo, :],
                            start=False, stop=True),
                          waits=[("sv", sv_pds)], inc=("st", 1))
                wdnb_free[wsl] = upd(wdnb_free[wsl], ("st", p.c("st")))
            st_dn = p.c("st")
            for j in range(4):
                bank = bankset[j]
                E("vector", lambda e, j=j, fo=fo, bk=bank: e.tensor_tensor(
                    out=xb[j][:, fo * 512:(fo + 1) * 512], in0=ps[bk][:],
                    in1=xb[j][:, fo * 512:(fo + 1) * 512], op=ALU.add),
                  waits=[("st", st_dn)], inc=("sv", 1))
                pfree[bank] = upd(pfree[bank], ("sv", p.c("sv")))
                if fo == 3:
                    E("gpsimd", lambda e, j=j, t=tiles[j]: e.dma_start(
                        out=d_out[t * 128:(t + 1) * 128, :], in_=xb[j][:]),
                      waits=[("sv", p.c("sv"))], inc=("gd", 16))
                    xb_free[j] = upd(xb_free[j], ("gd", p.c("gd")))
        sv_out = p.c("sv")
        ofm_free = upd(ofm_free, ("st", p.c("st")))
        xn2fm_free = [("st", p.c("st"))]
        hbuf_free = upd(hbuf_free, ("st", p.c("st")))
        if b + 1 < NB:
            emit_b_loads(b + 1)

    E("sync", lambda e: e.nop(), waits=[("sv", p.c("sv")), ("ss", p.c("ss")),
                                        ("st", p.c("st")), ("sg", p.c("sg")),
                                        ("gd", p.c("gd")), ("ld", p.c("ld"))])
    _emit(nc, p)
    return nc, p, tap_aps



def _prep_inputs(L0_attn_x, L0_attn_w_qkv, L0_attn_w_out, L0_attn_decay_param,
                 L0_ffn_W_router, L0_ffn_W_gate, L0_ffn_W_up, L0_ffn_W_down,
                 L0_ffn_sA_gate, L0_ffn_sB_gate, L0_ffn_sA_up, L0_ffn_sB_up,
                 L0_ffn_sA_down, L0_ffn_sB_down, L0_ffn_A_gate, L0_ffn_B_gate,
                 L0_ffn_A_up, L0_ffn_B_up, L0_ffn_A_down, L0_ffn_B_down,
                 L0_ffn_expert_bias):
    f = np.ascontiguousarray
    x = np.asarray(L0_attn_x, np.float32)
    wqkv = np.asarray(L0_attn_w_qkv, np.float32).reshape(16, 128, 12, 512).transpose(2, 1, 0, 3)
    wout = np.asarray(L0_attn_w_out, np.float32).reshape(16, 128, 4, 512).transpose(2, 1, 0, 3)
    wg = np.asarray(L0_ffn_W_gate, np.float32).reshape(16, 128, NSUB, 128).transpose(2, 1, 0, 3)
    wu = np.asarray(L0_ffn_W_up, np.float32).reshape(16, 128, NSUB, 128).transpose(2, 1, 0, 3)
    wdn = np.asarray(L0_ffn_W_down, np.float32).reshape(NSUB, 128, 4, 512).transpose(2, 1, 0, 3)

    acat_g = np.concatenate([np.asarray(L0_ffn_A_gate, np.float32).transpose(1, 0, 2).reshape(DM, 64),
                             np.asarray(L0_ffn_sA_gate, np.float32)], 1).reshape(16, 128, R9).transpose(1, 0, 2)
    acat_u = np.concatenate([np.asarray(L0_ffn_A_up, np.float32).transpose(1, 0, 2).reshape(DM, 64),
                             np.asarray(L0_ffn_sA_up, np.float32)], 1).reshape(16, 128, R9).transpose(1, 0, 2)
    acat_d = np.concatenate([np.asarray(L0_ffn_A_down, np.float32).transpose(1, 0, 2).reshape(DFF, 64),
                             np.asarray(L0_ffn_sA_down, np.float32)], 1)
    bcat_g = np.concatenate([np.asarray(L0_ffn_B_gate, np.float32).reshape(64, DFF),
                             np.asarray(L0_ffn_sB_gate, np.float32)], 0)
    bcat_u = np.concatenate([np.asarray(L0_ffn_B_up, np.float32).reshape(64, DFF),
                             np.asarray(L0_ffn_sB_up, np.float32)], 0)
    bcat_d = np.concatenate([np.asarray(L0_ffn_B_down, np.float32).reshape(64, DM),
                             np.asarray(L0_ffn_sB_down, np.float32)], 0).reshape(R9, 4, 512)

    acat_d_s = acat_d.reshape(NSUB, 128, R9)
    acat_d_s_host = acat_d_s.transpose(1, 0, 2)
    bcat_g_s = bcat_g.reshape(R9, NSUB, 128)
    bcat_u_s = bcat_u.reshape(R9, NSUB, 128)
    wgad = np.zeros((NSUB, 128, 2176), np.float32)
    wuad = np.zeros((NSUB, 128, 2176), np.float32)
    wgad[:, :, 0:2048] = wg.reshape(NSUB, 128, 2048)
    wuad[:, :, 0:2048] = wu.reshape(NSUB, 128, 2048)
    wgad[:, 0:R9, 2048:2176] = bcat_g_s.transpose(1, 0, 2)
    wuad[:, 0:R9, 2048:2176] = bcat_u_s.transpose(1, 0, 2)

    wrout = np.asarray(L0_ffn_W_router, np.float32).reshape(16, 128, 8).transpose(1, 0, 2)
    ebias = np.asarray(L0_ffn_expert_bias, np.float32).reshape(1, 8)
    dec = np.asarray(L0_attn_decay_param, np.float32).reshape(1, 16)
    idn_h = np.eye(128, dtype=ml_dtypes.bfloat16)
    idn_f = np.eye(128, dtype=np.float32)
    s8 = np.zeros((8, 64), ml_dtypes.bfloat16)
    for e in range(8):
        s8[e, e * 8:(e + 1) * 8] = 1

    shared = dict(wqkv=f(wqkv), wout=f(wout), wgad=f(wgad), wuad=f(wuad), wdn=f(wdn),
                  acat_g=f(acat_g), acat_u=f(acat_u), bcat_d=f(bcat_d), wrout=f(wrout),
                  acat_d=f(acat_d_s_host),
                  ebias=f(ebias), dec=f(dec), idn_h=idn_h, idn_f=idn_f, s8=s8)
    zeros_half = np.zeros((TOWN, DM), np.float32)
    in_maps = []
    for c in range(8):
        b, hh = c // 2, c % 2
        m = dict(shared)
        m["xo"] = f(x[b, hh * TOWN:(hh + 1) * TOWN])
        m["xp"] = f(x[b, 0:TOWN]) if hh == 1 else zeros_half
        m["bsel"] = np.full((1, 1), float(hh), np.float32)
        in_maps.append(m)
    return in_maps


def kernel(**inputs):
    in_maps = _prep_inputs(**inputs)
    nc, _, _ = build()
    res = run_bass_kernel_spmd(nc, in_maps, core_ids=list(range(8)))
    out = np.zeros((4, T, DM), np.float32)
    for c in range(8):
        b, hh = c // 2, c % 2
        out[b, hh * TOWN:(hh + 1) * TOWN] = np.asarray(res.results[c]["out"])
    return out
